# version 11
# baseline (speedup 1.0000x reference)
import os
import numpy as np
import concourse.bass as bass
import concourse.mybir as mybir
from concourse.bass_utils import run_bass_kernel_spmd

F32 = mybir.dt.float32
BF16 = mybir.dt.bfloat16
AF = mybir.ActivationFunctionType
AL = mybir.AluOpType
AX = mybir.AxisListType

B, L, D = 2, 8192, 1024
H, HD, WS = 16, 64, 64
O = 2048
HALO = 64
T = HALO + O
R = L - O
CSH = 20.0
NBLK = 17
PHASES = int(os.environ.get("MK_PHASES", "9"))
MKDBG = int(os.environ.get("MK_DEBUG", "0"))


def blk_nr(k):
    return 64 if k == 0 else 128


def blk_t0(k):
    return 0 if k == 0 else 64 + 128 * (k - 1)


class Ctx:
    def __init__(self, nc):
        self.nc = nc
        self.prog = {k: [] for k in ("sync", "vector", "tensor", "scalar", "gpsimd")}
        self.cnt = {}
        self.sem = {}
        self.pool = []

    def get_sem(self, name):
        if name not in self.sem:
            assert self.pool, "semaphore pool exhausted"
            self.sem[name] = self.pool.pop()
        return self.sem[name]

    def emit(self, eng, fn):
        self.prog[eng].append(fn)

    def bump(self, name, amt=1):
        self.cnt[name] = self.cnt.get(name, 0) + amt
        return self.cnt[name]

    def cur(self, name):
        return self.cnt.get(name, 0)

    def wait(self, eng, name, val):
        if val is None or val <= 0:
            return
        s = self.get_sem(name)
        self.emit(eng, lambda e, s=s, v=val: e.wait_ge(s, v))

    def waitd(self, eng, dep):
        if dep:
            self.wait(eng, dep[0], dep[1])

    def dma(self, queue, out, in_, sem="ld"):
        v = self.bump(sem, 16)
        s = self.get_sem(sem)
        self.emit(queue, lambda e, o=out, i=in_, s=s: e.dma_start(out=o, in_=i).then_inc(s, 16))
        return (sem, v)

    def op(self, eng, fn):
        sname = {"tensor": "PE", "vector": "VE", "scalar": "AC"}[eng]
        v = self.bump(sname)
        s = self.get_sem(sname)
        self.emit(eng, lambda e, f=fn, s=s: f(e).then_inc(s, 1))
        return (sname, v)

    def opq(self, eng, fn):
        self.emit(eng, fn)

    def fence_pe(self):
        self.wait("tensor", "PE", self.cur("PE"))


def build_program():
    nc = bass.Bass("TRN2", target_bir_lowering=False)
    g = {}
    for nm, shp, dt in (
        ("x_ownh", [T, D], F32), ("x_rest", [R, D], F32),
        ("w_v", [D, D], F32), ("b_v", [D], F32),
        ("w_wqk", [D, 2 * D], F32), ("b_wqk", [2 * D], F32),
        ("w_qk", [D, 2 * D], F32), ("b_qk", [2 * D], F32),
        ("w_out", [D, D], F32), ("b_out", [D], F32),
        ("f_wr", [D, D], F32), ("f_wk", [D, D], F32), ("f_wv", [D, D], F32),
        ("mu_r", [D], F32), ("mu_k", [D], F32),
        ("gb", [128, 64], F32), ("wbv", [128, 1], F32), ("hmask", [64, 1], F32),
        ("ident", [128, 128], BF16),
    ):
        g[nm] = nc.dram_tensor(nm, shp, dt, kind="ExternalInput").ap()
    g["y"] = nc.dram_tensor("y", [O, D], F32, kind="ExternalOutput").ap()
    if MKDBG:
        g["dbg1"] = nc.dram_tensor("dbg1", [128, 16896], BF16, kind="ExternalOutput").ap()

    g["kT_d"] = nc.dram_tensor("kT_d", [D, HALO + L], BF16).ap()
    g["qT_d"] = nc.dram_tensor("qT_d", [D, T], BF16).ap()
    g["v_d"] = nc.dram_tensor("v_d", [HALO + L, D], BF16).ap()
    g["w16"] = {}
    for nm, shp in (("wv", [D, D]), ("wwqk", [D, 2 * D]), ("wqk", [D, 2 * D]),
                    ("wout", [D, D]), ("wra", [D, D]), ("wrb", [D, D]),
                    ("wka", [D, D]), ("wkb", [D, D]), ("wvf", [D, D])):
        g["w16"][nm] = nc.dram_tensor("w16_" + nm, shp, BF16).ap()

    import contextlib
    with contextlib.ExitStack() as _st:
        en = _st.enter_context
        x_sb = en(nc.sbuf_tensor("x_sb", [128, NBLK * 1024], F32))
        A16 = en(nc.sbuf_tensor("A16", [128, 16896], BF16))
        B16 = en(nc.sbuf_tensor("B16", [128, 16896], BF16))
        SCR = en(nc.sbuf_tensor("SCR", [128, 33280], BF16))
        gbC = en(nc.sbuf_tensor("gbC", [128, 64], F32))
        wb1 = en(nc.sbuf_tensor("wb1", [128, 1], F32))
        hmask_s = en(nc.sbuf_tensor("hmask_s", [64, 1], F32))
        bqk_s = en(nc.sbuf_tensor("bqk_s", [128, 16], F32))
        bwqk_s = en(nc.sbuf_tensor("bwqk_s", [128, 16], F32))
        muT = en(nc.sbuf_tensor("muT", [128, 16], F32))
        ident16 = en(nc.sbuf_tensor("ident16", [128, 128], BF16))
        ones16 = en(nc.sbuf_tensor("ones16", [128, 8], BF16))
        onesf = en(nc.sbuf_tensor("onesf", [65, 128], F32))
        stats = en(nc.sbuf_tensor("stats", [128, 96], F32))
        r96 = en(nc.sbuf_tensor("r96", [1, 512], F32))
        stg16 = en(nc.sbuf_tensor("stg16", [128, 2048], BF16))
        recip_s = stg16.bitcast(F32)[:, 0:512]
        bc_sb = stg16.bitcast(F32)[:, 512:1024]
        PS = en(nc.psum_tensor("PS", [128, 4096], F32))
        sempool = [en(nc.semaphore(f"S{i}")) for i in range(40)]
        block = en(nc.Block())

        c = Ctx(nc)
        c.pool = sempool
        loc = locals()
        for nm in ("x_sb", "A16", "B16", "SCR", "gbC", "wb1", "hmask_s", "bqk_s",
                   "bwqk_s", "muT", "ident16", "ones16", "onesf", "stats",
                   "recip_s", "r96", "bc_sb", "stg16", "PS"):
            g[nm] = loc[nm]
        g["PS16"] = PS.bitcast(BF16)
        g["SCRF"] = SCR.bitcast(F32)
        g["A16F"] = A16.bitcast(F32)
        g["B16F"] = B16.bitcast(F32)

        build_all(c, g)

        @block.sync
        def _(e):
            for f in c.prog["sync"]:
                f(e)

        @block.vector
        def _(e):
            for f in c.prog["vector"]:
                f(e)

        @block.tensor
        def _(e):
            for f in c.prog["tensor"]:
                f(e)

        @block.scalar
        def _(e):
            for f in c.prog["scalar"]:
                f(e)

        @block.gpsimd
        def _(e):
            for f in c.prog["gpsimd"]:
                f(e)

    return nc


def build_all(c, g):
    x_sb, A16, B16, SCR = g["x_sb"], g["A16"], g["B16"], g["SCR"]
    PS, PS16, SCRF, B16F = g["PS"], g["PS16"], g["SCRF"], g["B16F"]
    stats, stg16 = g["stats"], g["stg16"]
    w16 = g["w16"]

    def bank(b, w=1):
        return PS[:, 512 * b:512 * (b + w)]

    def bank16(b, w=1):
        return PS16[:, 1024 * b:1024 * (b + w)]

    c.dma("sync", x_sb[0:64, 0:1024], g["x_ownh"][0:64, :], sem="ld_in")
    c.dma("sync", x_sb[:, 1024:].rearrange("p (k m) -> p k m", k=16),
          g["x_ownh"][64:T, :].rearrange("(k p) m -> p k m", p=128), sem="ld_in")
    c.dma("sync", g["gbC"][:, :], g["gb"], sem="ld_in")
    c.dma("sync", g["wb1"][:, :], g["wbv"], sem="ld_in")
    c.dma("sync", g["hmask_s"][:, :], g["hmask"], sem="ld_in")
    c.dma("sync", g["ident16"][:, :], g["ident"], sem="ld_in")
    d_in = ("ld_in", c.cur("ld_in"))
    d_xh = d_xo = d_cst = d_in

    c.waitd("vector", d_cst)
    v_stats0 = c.op("vector", lambda e: e.memset(g["stats"][:, :], 0.0))
    c.op("vector", lambda e: e.memset(g["ones16"][:, :], 1.0))
    c.op("vector", lambda e: e.memset(g["onesf"][:, :], 1.0))
    c.op("vector", lambda e: e.tensor_scalar_add(out=g["gbC"][:, :], in0=g["gbC"][:, :],
                                                 scalar1=-CSH))
    A16F, B16F = g["A16F"], g["B16F"]
    idf = B16F[:, 8192:8320]
    v_idf = c.op("vector", lambda e: e.tensor_copy(out=idf, in_=g["ident16"][:, :]))

    last_b7_rd = [None]

    def tr_small(dst, src_ap, n):
        stage = B16F[:, 8320:8448]
        dl = c.dma("sync", stage[0:n, 0:128], src_ap, sem="ld_st0")
        c.waitd("tensor", dl)
        c.waitd("tensor", v_idf)
        c.waitd("tensor", last_b7_rd[0])
        vp = c.op("tensor", lambda e, n=n: e.transpose(bank(7)[:, 0:n], stage[0:n, 0:128], idf[0:n, 0:n]))
        c.waitd("vector", vp)
        last_b7_rd[0] = c.op("vector", lambda e, d=dst, n=n: e.tensor_copy(
            out=d, in_=bank(7)[:, 0:n]))
        c.waitd("sync", vp)

    tr_small(g["bqk_s"][:, :], g["b_qk"].rearrange("(a p) -> a p", p=128), 16)
    tr_small(g["bwqk_s"][:, :], g["b_wqk"].rearrange("(a p) -> a p", p=128), 16)
    tr_small(g["muT"][:, 0:8], g["mu_r"].rearrange("(a p) -> a p", p=128), 8)
    tr_small(g["muT"][:, 8:16], g["mu_k"].rearrange("(a p) -> a p", p=128), 8)
    v_mu = last_b7_rd[0]

    bv_bc = SCRF[:, 14592:15616]
    bout_bc = SCRF[:, 15616:16640]

    def bias_bcast(dst, src_row):
        stage = SCRF[:, 11008:12032]
        c.wait("sync", "PE", c.cur("PE"))
        dl = c.dma("sync", stage[0:1, 0:1024], src_row.rearrange("(a m) -> a m", a=1),
                   sem="ld_bb")
        c.waitd("tensor", dl)
        for h in range(2):
            c.waitd("tensor", last_b7_rd[0])
            vp = c.op("tensor", lambda e, h=h: e.matmul(
                bank(7)[:, 0:512], g["onesf"][0:1, :], stage[0:1, 512 * h:512 * (h + 1)],
                start=True, stop=True))
            c.waitd("vector", vp)
            last_b7_rd[0] = c.op("vector", lambda e, h=h, d=dst: e.tensor_copy(
                out=d[:, 512 * h:512 * (h + 1)], in_=bank(7)[:, 0:512]))

    bias_bcast(bv_bc, g["b_v"])
    bias_bcast(bout_bc, g["b_out"])

    P0ST = {}

    def p0_main():
        stgf = [A16F[:, 2048 * i:2048 * (i + 1)] for i in range(4)]
        stcast = [None] * 4
        slot_cast = [None, None]
        slot_st = [None, None]
        wi = 0
        for nm, src, ncol in (("wv", g["w_v"], 1024), ("wwqk", g["w_wqk"], 2048),
                              ("wqk", g["w_qk"], 2048), ("wout", g["w_out"], 1024)):
            for ch in range(8):
                sl = wi % 4
                s = wi % 2
                wide = ncol == 2048
                c.waitd("sync", stcast[sl])
                dl = c.dma("sync", stgf[sl][:, 0:ncol], src[128 * ch:128 * (ch + 1), :],
                           sem="ld_sg%d" % sl)
                c.waitd("vector", dl)
                c.waitd("vector", slot_st[s])
                if wide:
                    c.waitd("vector", slot_st[1 - s])
                vcast = c.op("vector", lambda e, sl=sl, s=s, n=ncol: e.tensor_copy(
                    out=(stg16[:, 1024 * s:1024 * s + n] if n == 1024 else stg16[:, 0:2048]),
                    in_=stgf[sl][:, 0:n]))
                stcast[sl] = vcast
                slot_cast[s] = vcast
                if wide:
                    slot_cast[1 - s] = vcast
                c.waitd("scalar", vcast)
                stv = c.dma("scalar", w16[nm][128 * ch:128 * (ch + 1), :],
                            (stg16[:, 1024 * s:1024 * s + ncol] if ncol == 1024 else
                             stg16[:, 0:2048]), sem="st_w%d" % s)
                slot_st[s] = stv
                if wide:
                    slot_st[1 - s] = stv
                wi += 1

        WT = [B16F[:, 1024 * i:1024 * (i + 1)] for i in range(8)]
        one_m = stats[:, 85:86]
        c.waitd("vector", v_mu)
        ld_stage_rd = {}
        for src, nma, nmb, mu_off in ((g["f_wr"], "wra", "wrb", 0),
                                      (g["f_wk"], "wka", "wkb", 8),
                                      (g["f_wv"], "wvf", None, None)):
            for dch in range(8):
                sl = dch % 4
                c.waitd("sync", ld_stage_rd.get(sl))
                c.waitd("sync", stcast[sl])
                stcast[sl] = None
                dl = c.dma("sync", stgf[sl][:, 0:1024], src[128 * dch:128 * (dch + 1), :],
                           sem="ld_sg%d" % sl)
                c.waitd("tensor", dl)
                for c4 in range(2):
                    c.waitd("tensor", last_b7_rd[0])
                    for cc in range(4 * c4, 4 * c4 + 4):
                        vp = c.op("tensor", lambda e, cc=cc, sl=sl, c4=c4: e.transpose(
                            bank(7)[:, 128 * (cc - 4 * c4):128 * (cc - 4 * c4) + 128],
                            stgf[sl][:, 128 * cc:128 * (cc + 1)], idf))
                    c.waitd("vector", vp)
                    dst4 = B16F[:, 0:8192].rearrange("p (cc m) -> p cc m", cc=8)[
                        :, 4 * c4:4 * c4 + 4, 128 * dch:128 * (dch + 1)]
                    last_b7_rd[0] = c.op("vector", lambda e, d=dst4: e.tensor_copy(
                        out=d, in_=bank(7).rearrange("p (cc m) -> p cc m", cc=4)))
                ld_stage_rd[sl] = vp
            c.waitd("vector", last_b7_rd[0])
            for cc in range(8):
                if nmb is not None:
                    c.waitd("vector", slot_st[0])
                    slot_cast[0] = c.op("vector", lambda e, cc=cc, o=mu_off: e.tensor_scalar(
                        out=stg16[:, 0:1024], in0=WT[cc], scalar1=g["muT"][:, o + cc:o + cc + 1],
                        scalar2=None, op0=AL.mult))
                    c.waitd("scalar", slot_cast[0])
                    slot_st[0] = c.dma("scalar", w16[nma][128 * cc:128 * (cc + 1), :],
                                       stg16[:, 0:1024], sem="st_w0")
                    c.waitd("vector", slot_cast[1])
                    vom = c.op("vector", lambda e, o=mu_off, cc=cc: e.tensor_scalar(
                        out=one_m, in0=g["muT"][:, o + cc:o + cc + 1], scalar1=-1.0, scalar2=1.0,
                        op0=AL.mult, op1=AL.add))
                    c.waitd("vector", vom)
                    c.waitd("vector", slot_st[1])
                    slot_cast[1] = c.op("vector", lambda e, cc=cc: e.tensor_scalar(
                        out=stg16[:, 1024:2048], in0=WT[cc], scalar1=one_m,
                        scalar2=None, op0=AL.mult))
                    c.waitd("scalar", slot_cast[1])
                    slot_st[1] = c.dma("scalar", w16[nmb][128 * cc:128 * (cc + 1), :],
                                       stg16[:, 1024:2048], sem="st_w1")
                else:
                    c.waitd("vector", slot_st[0])
                    slot_cast[0] = c.op("vector", lambda e, cc=cc: e.tensor_copy(
                        out=stg16[:, 0:1024], in_=WT[cc]))
                    c.waitd("scalar", slot_cast[0])
                    slot_st[0] = c.dma("scalar", w16["wvf"][128 * cc:128 * (cc + 1), :],
                                       stg16[:, 0:1024], sem="st_w0")

        p0_sd = [("st_w0", c.cur("st_w0")), ("st_w1", c.cur("st_w1"))]
        p0_ve = ("VE", c.cur("VE"))
        g_slotst = {0: slot_st[0], 1: slot_st[1]}
        P0ST.update(p0_sd=p0_sd, p0_ve=p0_ve, g_slotst=g_slotst)


    def X1T(cc, t0, tw):
        return SCR[:, 2112 * cc + t0:2112 * cc + t0 + tw]

    def ln_pass(nblks, nr_fn, src_fn, apply_cb, xdeps=(), junk_col=20992):
        first = True
        for k in range(nblks):
            nr = nr_fn(k)
            xb = src_fn(k)
            if first:
                for dpp in xdeps:
                    c.waitd("scalar", dpp)
                    c.waitd("vector", dpp)
                first = False
            if k > 0:
                c.wait("scalar", "AC", c.cur("AC"))
            c.op("scalar", lambda e, xb=xb, k=k, nr=nr: e.activation(
                out=SCR[0:nr, junk_col:junk_col + 1024], in_=xb, func=AF.Square,
                accum_out=stats[0:nr, 17 + k:18 + k]))
            c.op("vector", lambda e, xb=xb, k=k, nr=nr: e.reduce_sum(
                out=stats[0:nr, k:k + 1], in_=xb, axis=AX.X))
        c.wait("vector", "AC", c.cur("AC"))
        c.wait("vector", "VE", c.cur("VE"))
        n = nblks

        def vop(fn):
            c.wait("vector", "VE", c.cur("VE"))
            return c.op("vector", fn)

        vop(lambda e, n=n: e.tensor_scalar(
            out=stats[:, 34:34 + n], in0=stats[:, 0:n], scalar1=1.0 / D, scalar2=None,
            op0=AL.mult))
        vop(lambda e, n=n: e.tensor_scalar(
            out=stats[:, 51:51 + n], in0=stats[:, 17:17 + n], scalar1=1.0 / D, scalar2=None,
            op0=AL.mult))
        vop(lambda e, n=n: e.tensor_tensor(
            out=stats[:, 68:68 + n], in0=stats[:, 34:34 + n], in1=stats[:, 34:34 + n],
            op=AL.mult))
        vop(lambda e, n=n: e.tensor_tensor(
            out=stats[:, 51:51 + n], in0=stats[:, 51:51 + n], in1=stats[:, 68:68 + n],
            op=AL.subtract))
        vop(lambda e, n=n: e.tensor_scalar_add(
            out=stats[:, 51:51 + n], in0=stats[:, 51:51 + n], scalar1=1e-5))
        vv = vop(lambda e, n=n: e.reciprocal(
            out=stats[:, 68:68 + n], in_=stats[:, 51:51 + n]))
        c.waitd("scalar", vv)
        va = c.op("scalar", lambda e, n=n: e.activation(
            out=stats[:, 51:51 + n], in_=stats[:, 68:68 + n], func=AF.Sqrt))
        c.waitd("vector", va)
        vop(lambda e, n=n: e.scalar_tensor_tensor(
            out=stats[:, 68:68 + n], in0=stats[:, 34:34 + n], scalar=-1.0,
            in1=stats[:, 51:51 + n], op0=AL.mult, op1=AL.mult))
        c.wait("vector", "VE", c.cur("VE"))
        for k in range(nblks):
            apply_cb(k, nr_fn(k), 51 + k, 68 + k)

    def to_x1t(src_ap, nr, dst3_ap, dep):
        c.waitd("tensor", dep)
        c.waitd("tensor", last_b7_rd[0])
        vp = None
        for cc in range(8):
            vp = c.op("tensor", lambda e, cc=cc, nr=nr, a=src_ap: e.transpose(
                bank16(7)[:, 128 * cc:128 * cc + nr],
                a[0:nr, 128 * cc:128 * (cc + 1)], g["ident16"][0:nr, 0:nr]))
        c.waitd("vector", vp)
        vv = c.op("vector", lambda e, o=dst3_ap, nr=nr: e.tensor_copy(
            out=o, in_=bank16(7).rearrange("p (cc t) -> p cc t", cc=8)[:, :, 0:nr]))
        last_b7_rd[0] = vv
        return vv

    _lwn = [0]

    def load_w(nm, arena, col0, nch, ncol, src_col0=0, src_ncol=None, deps=()):
        for dpp in deps:
            c.waitd("sync", dpp)
        _lwn[0] += 1
        sem = "ld_w%d" % _lwn[0]
        dl = None
        if src_ncol is None:
            src_ncol = ncol
        for ch in range(nch):
            dl = c.dma("sync", arena[:, col0 + ncol * ch:col0 + ncol * ch + src_ncol],
                       w16[nm][128 * ch:128 * (ch + 1), src_col0:src_col0 + src_ncol],
                       sem=sem)
        return dl

    x1t_dep = [None] * NBLK

    def p1a():
        tr_dep = [None] * NBLK

        def apply_cb(k, nr, sc, bi):
            col = 16896 + 1024 * (k % 4)
            if k >= 4:
                c.waitd("vector", tr_dep[k - 4])
            vv = c.op("vector", lambda e, k=k, nr=nr, col=col, sc=sc, bi=bi: e.tensor_scalar(
                out=SCR[0:nr, col:col + 1024], in0=x_sb[0:nr, 1024 * k:1024 * (k + 1)],
                scalar1=stats[0:nr, sc:sc + 1], scalar2=stats[0:nr, bi:bi + 1],
                op0=AL.mult, op1=AL.add))
            dst = SCR[:, 0:16896].rearrange("p (cc t) -> p cc t", cc=8)[
                :, :, blk_t0(k):blk_t0(k) + nr]
            tr_dep[k] = to_x1t(SCR[:, col:col + 1024], nr, dst, vv)
            x1t_dep[k] = tr_dep[k]

        ln_pass(NBLK, blk_nr, lambda k: x_sb[0:blk_nr(k), 1024 * k:1024 * (k + 1)],
                apply_cb, xdeps=(d_xh, d_xo, v_stats0))

    def p1b():
        c.wait("tensor", "VE", c.cur("VE"))
        c.wait("tensor", "AC", c.cur("AC"))
        wdep = load_w("wv", B16, 0, 8, 1024, deps=tuple(P0ST["p0_sd"]))
        c.waitd("tensor", wdep)
        for k in range(NBLK):
            c.waitd("tensor", x1t_dep[k])
        ps_rd = [None, None]
        st_dep = {-2: g_slotst[0], -1: g_slotst[1]}
        for k in range(NBLK):
            nr, t0 = blk_nr(k), blk_t0(k)
            pb = (k % 2) * 2
            c.waitd("tensor", ps_rd[k % 2])
            vp = None
            for dh in range(2):
                for cc in range(8):
                    vp = c.op("tensor", lambda e, nr=nr, t0=t0, pb=pb, dh=dh, cc=cc:
                              e.matmul(bank(pb + dh)[0:nr, :], X1T(cc, t0, nr),
                                       B16[:, 1024 * cc + 512 * dh:1024 * cc + 512 * (dh + 1)],
                                       start=(cc == 0), stop=(cc == 7)))
            c.waitd("vector", vp)
            s = k % 2
            c.waitd("vector", st_dep.get(k - 2))
            vv = c.op("vector", lambda e, nr=nr, pb=pb, s=s: e.tensor_tensor(
                out=stg16[0:nr, 1024 * s:1024 * (s + 1)], in0=bank(pb, 2)[0:nr, 0:1024],
                in1=bv_bc[0:nr, :], op=AL.add))
            ps_rd[k % 2] = vv
            q = "scalar" if s == 0 else "gpsimd"
            c.waitd(q, vv)
            st_dep[k] = c.dma(q, g["v_d"][t0:t0 + nr, :],
                              stg16[0:nr, 1024 * s:1024 * (s + 1)],
                              sem=("st_w0" if s == 0 else "sg_w1"))
            g_slotst[s] = st_dep[k]

    def p1c():
        c.wait("tensor", "VE", c.cur("VE"))
        c.wait("tensor", "AC", c.cur("AC"))
        wdep = load_w("wqk", B16, 0, 8, 2048, deps=(("PE", c.cur("PE")),))
        c.waitd("tensor", wdep)
        ttiles = [(0, 512), (512, 512), (1024, 512), (1536, 512), (2048, 64)]
        ps_rd = [None] * 4
        st_dep = {-4: g_slotst[0], -3: g_slotst[1], -2: g_slotst[0], -1: g_slotst[1]}
        c.wait("vector", "st_w0", c.cur("st_w0"))
        c.wait("vector", "st_w1", c.cur("st_w1"))
        i = 0
        for dch in range(16):
            for (t0, tw) in ttiles:
                pb = i % 4
                c.waitd("tensor", ps_rd[pb])
                vp = None
                for cc in range(8):
                    vp = c.op("tensor", lambda e, dch=dch, t0=t0, tw=tw, pb=pb, cc=cc:
                              e.matmul(bank(pb)[:, 0:tw],
                                       B16[:, 2048 * cc + 128 * dch:2048 * cc + 128 * (dch + 1)],
                                       X1T(cc, t0, tw), start=(cc == 0), stop=(cc == 7)))
                c.waitd("vector", vp)
                s = i % 4
                c.waitd("vector", st_dep.get(i - 4))
                vv = c.op("vector", lambda e, dch=dch, tw=tw, pb=pb, s=s: e.tensor_scalar(
                    out=stg16[:, 512 * s:512 * s + tw], in0=bank(pb)[:, 0:tw],
                    scalar1=g["bqk_s"][:, dch:dch + 1], scalar2=None, op0=AL.add))
                ps_rd[pb] = vv
                q = "scalar" if s % 2 == 0 else "gpsimd"
                c.waitd(q, vv)
                if dch < 8:
                    dst = g["qT_d"][128 * dch:128 * (dch + 1), t0:t0 + tw]
                else:
                    dst = g["kT_d"][128 * (dch - 8):128 * (dch - 7), t0:t0 + tw]
                st_dep[i] = c.dma(q, dst, stg16[:, 512 * s:512 * s + tw],
                                  sem=("st_w%d" % s if s % 2 == 0 else "sg_w%d" % s))
                g_slotst[s % 2] = st_dep[i]
                i += 1

    def p1d():
        c.wait("tensor", "VE", c.cur("VE"))
        c.wait("tensor", "AC", c.cur("AC"))
        c.wait("vector", "st_w0", c.cur("st_w0"))
        c.wait("vector", "st_w1", c.cur("st_w1"))
        wdep = load_w("wwqk", B16, 0, 8, 2048, deps=(("PE", c.cur("PE")),))
        wodep = load_w("wout", A16, 0, 8, 1024, deps=(P0ST["p0_ve"],))
        c.waitd("sync", g_slotst[0])
        c.waitd("sync", g_slotst[1])
        wtiles = [(64 + 256 * i, 256) for i in range(8)] + [(0, 64)]
        if os.environ.get("MK_NOHALO"):
            wtiles = wtiles[:-1]
        arena_rd = [None]
        pw_rd = [None, None]
        s_exp = [None, None]
        wn_rd = [None, None]
        wtn_rd = [None]
        oproj_rd = [None, None]
        for (t0, tw) in wtiles:
            nwin = tw // 64
            c.waitd("sync", arena_rd[0])
            if tw == 256:
                dv = c.dma("sync", A16[:, 8192:10240].rearrange("p (s m) -> p s m", m=1024),
                           g["v_d"][t0:t0 + tw, :].rearrange("(s p) m -> p s m", p=128),
                           sem="ld_vt")
            else:
                dv = c.dma("sync", A16[0:64, 8192:9216], g["v_d"][t0:t0 + 64, :],
                           sem="ld_vt")
            ps_rd = [oproj_rd[0], oproj_rd[1]]
            wq_done = None
            for dch in range(16):
                pb = dch % 2
                c.waitd("tensor", ps_rd[pb])
                if dch == 0:
                    c.waitd("tensor", wdep)
                    c.waitd("tensor", arena_rd[0])
                vp = None
                for cc in range(8):
                    vp = c.op("tensor", lambda e, dch=dch, t0=t0, tw=tw, pb=pb, cc=cc:
                              e.matmul(bank(pb)[:, 0:tw],
                                       B16[:, 2048 * cc + 128 * dch:2048 * cc + 128 * (dch + 1)],
                                       X1T(cc, t0, tw), start=(cc == 0), stop=(cc == 7)))
                c.waitd("vector", vp)
                vv = c.op("vector", lambda e, dch=dch, tw=tw, pb=pb: e.tensor_scalar(
                    out=A16[:, 12288 + 256 * dch:12288 + 256 * dch + tw], in0=bank(pb)[:, 0:tw],
                    scalar1=g["bwqk_s"][:, dch:dch + 1], scalar2=None, op0=AL.add))
                ps_rd[pb] = vv
                wq_done = vv
            c.waitd("tensor", wq_done)
            c.waitd("tensor", dv)

            P1D = int(os.environ.get("MK_P1D", "5"))
            if P1D < 2:
                arena_rd[0] = wq_done
                continue
            vz = c.op("vector", lambda e: e.memset(bank(2, 4)[:, :], 0.0))
            vz2 = c.op("vector", lambda e: e.memset(bank(6)[:, 0:32], 0.0))
            c.waitd("tensor", vz)
            c.waitd("tensor", vz2)

            def qcol(hp):
                return 12288 + 256 * hp

            def kcol(hp):
                return 12288 + 256 * (8 + hp)

            if nwin == 4:
                last_pv = None
                for g2 in range(4):
                    sb = g2 % 2
                    c.waitd("tensor", s_exp[sb])
                    vp = None
                    for h_i in range(2):
                        for hp_i, hp in enumerate((2 * g2, 2 * g2 + 1)):
                            h = 2 * hp + h_i
                            r = 64 * h_i
                            for wp in range(2):
                                slot = 256 * hp_i + 128 * h_i + 64 * wp
                                for w_i, w in enumerate((2 * wp, 2 * wp + 1)):
                                    cc2 = 64 * w_i
                                    vp = c.op("tensor", lambda e, r=r, cc2=cc2, w=w, sb=sb,
                                              slot=slot, hp=hp, h_i=h_i:
                                              e.matmul(
                                                  bank(sb)[cc2:cc2 + 64, slot:slot + 64],
                                                  A16[r:r + 64, kcol(hp) + 64 * w:
                                                      kcol(hp) + 64 * (w + 1)],
                                                  A16[r:r + 64, qcol(hp) + 64 * w:
                                                      qcol(hp) + 64 * (w + 1)],
                                                  start=True, stop=True,
                                                  tile_position=(r, cc2)))
                        c.fence_pe()
                    if P1D < 3:
                        continue
                    c.waitd("scalar", vp)
                    pwc = 16896 + 512 * (g2 % 2)
                    c.waitd("scalar", pw_rd[g2 % 2])
                    va = c.op("scalar", lambda e, sb=sb, pwc=pwc: e.activation(
                        out=SCR[:, pwc:pwc + 512], in_=bank(sb)[:, 0:512], func=AF.Exp,
                        bias=g["wb1"][:, :], scale=0.125))
                    s_exp[sb] = va
                    c.waitd("tensor", va)
                    for hp_i, hp in enumerate((2 * g2, 2 * g2 + 1)):
                        for h_i in range(2):
                            h = 2 * hp + h_i
                            for wp in range(2):
                                slot = 256 * hp_i + 128 * h_i + 64 * wp
                                ob = 2 + 2 * wp
                                for w_i, w in enumerate((2 * wp, 2 * wp + 1)):
                                    r = 64 * w_i
                                    vbase = 8192 + 1024 * (w // 2)
                                    vp = c.op("tensor", lambda e, r=r, w=w, h=h, slot=slot,
                                              pwc=pwc, ob=ob, vbase=vbase:
                                              e.matmul(
                                                  PS[r:r + 64,
                                                     512 * (ob + (h >= 8)) + (64 * h) % 512:
                                                     512 * (ob + (h >= 8)) + (64 * h) % 512 + 64],
                                                  SCR[r:r + 64, pwc + slot:pwc + slot + 64],
                                                  A16[r:r + 64, vbase + 64 * h:
                                                      vbase + 64 * (h + 1)],
                                                  start=False, stop=True,
                                                  skip_group_check=True,
                                                  tile_position=(r, r)))
                                    vp2 = c.op("tensor", lambda e, r=r, h=h, slot=slot,
                                               pwc=pwc, wp=wp:
                                               e.matmul(
                                                   PS[r:r + 64, 512 * 6 + 16 * wp + h:
                                                      512 * 6 + 16 * wp + h + 1],
                                                   SCR[r:r + 64, pwc + slot:pwc + slot + 64],
                                                   g["ones16"][r:r + 64, 0:1],
                                                   start=False, stop=True,
                                                   skip_group_check=True,
                                                   tile_position=(r, r)))
                                    last_pv = vp2
                    pw_rd[g2 % 2] = last_pv
                    c.fence_pe()
                pv_all = last_pv
            else:
                HSTG = int(os.environ.get("MK_HALOSTAGE", "5"))
                if HSTG < 2:
                    arena_rd[0] = wq_done
                    continue
                vp = None
                c.waitd("tensor", s_exp[0])
                for h in range(16):
                    hp, h_i = h // 2, h % 2
                    r = 64 * h_i
                    ch = 512 * (h % 2) + 64 * (h // 2)
                    vp = c.op("tensor", lambda e, h=h, hp=hp, r=r, ch=ch: e.matmul(
                        bank(0, 2)[0:64, ch:ch + 64],
                        A16[r:r + 64, kcol(hp):kcol(hp) + 64],
                        A16[r:r + 64, qcol(hp):qcol(hp) + 64],
                        start=True, stop=True, tile_position=(r, 0)))
                if HSTG < 3:
                    arena_rd[0] = vp
                    continue
                c.waitd("scalar", vp)
                c.waitd("scalar", pw_rd[0])
                va = c.op("scalar", lambda e: e.activation(
                    out=SCR[0:64, 16896:17920], in_=bank(0, 2)[0:64, 0:1024], func=AF.Exp,
                    bias=g["wb1"][0:64, :], scale=0.125))
                s_exp[0] = va
                s_exp[1] = va
                if HSTG == 3 and False:
                    pass
                c.waitd("tensor", va)
                for h in range(16):
                    ob = 2 + (h >= 8)
                    ch = 512 * (h % 2) + 64 * (h // 2)
                    vp = c.op("tensor", lambda e, h=h, ob=ob, ch=ch: e.matmul(
                        PS[0:64, 512 * ob + (64 * h) % 512:512 * ob + (64 * h) % 512 + 64],
                        SCR[0:64, 16896 + ch:16896 + ch + 64],
                        A16[0:64, 8192 + 64 * h:8192 + 64 * (h + 1)],
                        start=False, stop=True, skip_group_check=True,
                        tile_position=(0, 0)))
                    vp2 = c.op("tensor", lambda e, h=h, ch=ch: e.matmul(
                        PS[0:64, 512 * 6 + h:512 * 6 + h + 1],
                        SCR[0:64, 16896 + ch:16896 + ch + 64],
                        g["ones16"][0:64, 0:1],
                        start=False, stop=True, skip_group_check=True,
                        tile_position=(0, 0)))
                pw_rd[0] = vp2
                pw_rd[1] = vp2
                pv_all = vp2
                if HSTG < 4:
                    arena_rd[0] = pv_all
                    continue

            if P1D < 3:
                arena_rd[0] = vp
                continue
            if P1D < 4:
                arena_rd[0] = pv_all
                continue
            c.waitd("vector", pv_all)
            nwp = max(1, nwin // 2)
            vrec = c.op("vector", lambda e: e.reciprocal(
                out=g["recip_s"][:, 0:32], in_=bank(6)[:, 0:32]))
            c.waitd("vector", vrec)
            tdep = {}
            for wp in range(nwp):
                wnc = 17920 + 1024 * wp
                nq = 128 if nwin == 4 else 64
                c.waitd("vector", wn_rd[wp])
                rbc = g["recip_s"][:, 16 * wp:16 * wp + 16].unsqueeze(2).broadcast_to(
                    (128, 16, 64))
                vv = c.op("vector", lambda e, wp=wp, wnc=wnc, nq=nq, rbc=rbc: e.tensor_tensor(
                    out=SCR[0:nq, wnc:wnc + 1024].rearrange("p (h d) -> p h d", h=16),
                    in0=bank(2 + 2 * wp, 2)[0:nq, 0:1024].rearrange("p (h d) -> p h d", h=16),
                    in1=rbc[0:nq], op=AL.mult))
                dst = SCR[:, 20992:23040].rearrange("p (cc t) -> p cc t", cc=8)[
                    :, :, 128 * wp:128 * wp + nq]
                tdep[wp] = to_x1t(SCR[:, wnc:wnc + 1024], nq, dst, vv)
                wn_rd[wp] = tdep[wp]
            if P1D < 5:
                arena_rd[0] = pv_all
                continue
            c.waitd("tensor", tdep[0])
            if nwp > 1:
                c.waitd("tensor", tdep[nwp - 1])
            c.waitd("tensor", wodep)
            nsub = 2 if nwin == 4 else 1
            for ts in range(nsub):
                nq = 128 if nwin == 4 else 64
                for df in range(2):
                    pb = df
                    c.waitd("tensor", oproj_rd[pb])
                    vp = None
                    for cc in range(8):
                        vp = c.op("tensor", lambda e, ts=ts, df=df, cc=cc, nq=nq, pb=pb:
                                  e.matmul(bank(pb)[0:nq, :],
                                           SCR[:, 20992 + 256 * cc + 128 * ts:
                                               20992 + 256 * cc + 128 * ts + nq],
                                           A16[:, 1024 * cc + 512 * df:
                                               1024 * cc + 512 * (df + 1)],
                                           start=(cc == 0), stop=(cc == 7)))
                    c.waitd("vector", vp)
                    c.wait("vector", "VE", c.cur("VE"))
                    tt = t0 + 128 * ts
                    kblk = 0 if tt < 64 else 1 + (tt - 64) // 128
                    vv = c.op("vector", lambda e, pb=pb, df=df, nq=nq: e.tensor_tensor(
                        out=g["bc_sb"][0:nq, :], in0=bank(pb)[0:nq, :],
                        in1=bout_bc[0:nq, 512 * df:512 * (df + 1)], op=AL.add))
                    oproj_rd[pb] = vv
                    c.waitd("vector", vv)
                    vv = c.op("vector", lambda e, kblk=kblk, df=df, nq=nq: e.tensor_tensor(
                        out=x_sb[0:nq, 1024 * kblk + 512 * df:1024 * kblk + 512 * (df + 1)],
                        in0=x_sb[0:nq, 1024 * kblk + 512 * df:1024 * kblk + 512 * (df + 1)],
                        in1=g["bc_sb"][0:nq, :], op=AL.add))
                    wtn_rd[0] = vp
            arena_rd[0] = vp
            s_exp[0] = oproj_rd[0]
            s_exp[1] = oproj_rd[1]
        p1d_x = ("VE", c.cur("VE"))
        return p1d_x

    def p2():
        c.wait("tensor", "VE", c.cur("VE"))
        c.wait("tensor", "AC", c.cur("AC"))
        pe_guard = ("PE", c.cur("PE"))
        wdep = load_w("wqk", B16, 0, 8, 1024, src_col0=1024, src_ncol=1024,
                      deps=(pe_guard,))
        wdep2 = load_w("wv", B16, 8192, 8, 1024)
        xst = SCRF[:, 4096:12288]

        def xblk(bk):
            return xst[:, 2048 * bk:2048 * bk + 1024]

        prev_use = [None]
        st_dep = {-2: g_slotst[0], -1: g_slotst[1]}
        ps_rd_v = [None, None]
        i = [0]
        for rt in range(12):
            c.waitd("sync", prev_use[0])
            c.wait("sync", "AC", c.cur("AC"))
            dx = c.dma("sync", xst.rearrange("p (bk m) -> p bk m", bk=4)[:, :, 0:1024],
                       g["x_rest"][512 * rt:512 * (rt + 1), :].rearrange(
                           "(bk p) m -> p bk m", p=128), sem="ld_xt")
            tr_dep = {}

            def apply_cb(k, nr, sc, bi, rt=rt):
                col = 24576 + 1024 * k
                vv = c.op("vector", lambda e, k=k, col=col, sc=sc, bi=bi: e.tensor_scalar(
                    out=SCR[:, col:col + 1024], in0=xblk(k),
                    scalar1=stats[:, sc:sc + 1], scalar2=stats[:, bi:bi + 1],
                    op0=AL.mult, op1=AL.add))
                dst = SCR[:, 0:4096].rearrange("p (cc t) -> p cc t", cc=8)[
                    :, :, 128 * k:128 * (k + 1)]
                tr_dep[k] = to_x1t(SCR[:, col:col + 1024], 128, dst, vv)

            ln_pass(4, lambda k: 128, xblk, apply_cb,
                    xdeps=(dx, ("PE", c.cur("PE"))), junk_col=28672 - 1024)
            c.waitd("tensor", tr_dep[3])
            prev_use[0] = tr_dep[3]

            def X1R(cc, tw=512):
                return SCR[:, 512 * cc:512 * cc + tw]

            ps_rd = [None, None]
            for dch in range(8):
                pb = i[0] % 2
                c.waitd("tensor", ps_rd[pb])
                if rt == 0 and dch == 0:
                    c.waitd("tensor", wdep)
                    c.waitd("tensor", wdep2)
                vp = None
                for cc in range(8):
                    vp = c.op("tensor", lambda e, dch=dch, pb=pb, cc=cc:
                              e.matmul(bank(pb)[:, 0:512],
                                       B16[:, 1024 * cc + 128 * dch:1024 * cc + 128 * (dch + 1)],
                                       X1R(cc), start=(cc == 0), stop=(cc == 7)))
                c.waitd("vector", vp)
                s = i[0] % 2
                c.waitd("vector", st_dep.get(i[0] - 2))
                vv = c.op("vector", lambda e, dch=dch, pb=pb, s=s: e.tensor_scalar(
                    out=stg16[:, 1024 * s:1024 * s + 512], in0=bank(pb)[:, 0:512],
                    scalar1=g["bqk_s"][:, 8 + dch:9 + dch], scalar2=None, op0=AL.add))
                ps_rd[pb] = vv
                q = "scalar" if s == 0 else "gpsimd"
                c.waitd(q, vv)
                st_dep[i[0]] = c.dma(
                    q, g["kT_d"][128 * dch:128 * (dch + 1),
                                 T + 512 * rt:T + 512 * (rt + 1)],
                    stg16[:, 1024 * s:1024 * s + 512],
                    sem=("st_w0" if s == 0 else "sg_w1"))
                g_slotst[s] = st_dep[i[0]]
                i[0] += 1
            for bk in range(4):
                pb2 = 2 + (bk % 2) * 2
                c.waitd("tensor", ps_rd_v[bk % 2])
                vp = None
                for dh in range(2):
                    for cc in range(8):
                        vp = c.op("tensor", lambda e, bk=bk, pb2=pb2, dh=dh, cc=cc:
                                  e.matmul(bank(pb2 + dh)[:, :],
                                           SCR[:, 512 * cc + 128 * bk:512 * cc + 128 * (bk + 1)],
                                           B16[:, 8192 + 1024 * cc + 512 * dh:
                                               8192 + 1024 * cc + 512 * (dh + 1)],
                                           start=(cc == 0), stop=(cc == 7)))
                c.waitd("vector", vp)
                s = i[0] % 2
                c.waitd("vector", st_dep.get(i[0] - 2))
                vv = c.op("vector", lambda e, pb2=pb2, s=s: e.tensor_tensor(
                    out=stg16[:, 1024 * s:1024 * (s + 1)], in0=bank(pb2, 2)[:, 0:1024],
                    in1=bv_bc[:, :], op=AL.add))
                ps_rd_v[bk % 2] = vv
                q = "scalar" if s == 0 else "gpsimd"
                c.waitd(q, vv)
                st_dep[i[0]] = c.dma(
                    q, g["v_d"][T + 512 * rt + 128 * bk:T + 512 * rt + 128 * (bk + 1), :],
                    stg16[:, 1024 * s:1024 * (s + 1)],
                    sem=("st_w0" if s == 0 else "sg_w1"))
                g_slotst[s] = st_dep[i[0]]
                i[0] += 1
        return ([("st_w%d" % sw, c.cur("st_w%d" % sw)) for sw in range(4)] +
                [(nm, c.cur(nm)) for nm in ("sg_w1", "sg_w3")])

    def p3(p2_done):
        c.wait("tensor", "VE", c.cur("VE"))
        c.wait("tensor", "AC", c.cur("AC"))
        qdep = None
        c.wait("sync", "PE", c.cur("PE"))
        for ch in range(8):
            qdep = c.dma("sync", A16[:, 2112 * ch:2112 * (ch + 1)],
                         g["qT_d"][128 * ch:128 * (ch + 1), :], sem="ld_qt")
        for dpp in p2_done:
            c.waitd("sync", dpp)
            c.waitd("gpsimd", dpp)
        ichunks = [(0, 512), (512, 512), (1024, 512), (1536, 512), (2048, 64)]
        NJ = 64
        PD = 4
        aoT_dep = [None]
        mset_dep = [None]
        pending_fin = [None]
        vph_rd = [None]
        ktbuf_lastrd = [None, None]
        slot_rd = {}
        exp_cnt = {}

        NPH = int(os.environ.get("MK_NPH", "4"))
        for ph in range(NPH):
            c.waitd("gpsimd", vph_rd[0])
            vph_ld = []
            for s8 in range(8):
                vph_ld.append(c.dma("gpsimd",
                              SCR[:, 16384 + 2048 * s8:16384 + 2048 * (s8 + 1)].rearrange(
                                  "p (j m) -> p j m", j=8),
                              g["v_d"][HALO + 1024 * s8:HALO + 1024 * (s8 + 1),
                                       256 * ph:256 * (ph + 1)].rearrange(
                                  "(j p) m -> p j m", p=128), sem="ld_vph"))
            vdep = vph_ld[-1]
            c.waitd("tensor", vdep)
            if ph == 0:
                c.waitd("tensor", qdep)
                c.wait("vector", "VE", c.cur("VE"))
                vz = c.op("vector", lambda e: e.memset(bank(4, 2)[:, :], 0.0))
                vz2 = c.op("vector", lambda e: e.memset(bank(6)[:, :], 0.0))
                mset_dep[0] = vz2
                c.waitd("tensor", vz)
                c.waitd("tensor", vz2)

            for ii, (i0, iw) in enumerate(ichunks):
                kt_ld = {}

                def load_quarter(q, ph=ph):
                    bsel = q % 2
                    c.waitd("sync", ktbuf_lastrd[bsel])
                    dl = None
                    for chh in range(2):
                        dl = c.dma("sync",
                                   SCR[:, 8192 + 4096 * bsel + 2048 * chh:
                                       8192 + 4096 * bsel + 2048 * (chh + 1)],
                                   g["kT_d"][128 * (2 * ph + chh):128 * (2 * ph + chh + 1),
                                             HALO + 2048 * q:HALO + 2048 * (q + 1)],
                                   sem="ld_kt%d" % bsel)
                    kt_ld[q] = dl

                load_quarter(0)
                load_quarter(1)

                def kt_ap(s, j):
                    q, jj = j // 16, j % 16
                    base = 8192 + 4096 * (q % 2) + 2048 * s
                    return base + 128 * jj

                def emit_qk(s, j):
                    q = j // 16
                    c.waitd("tensor", kt_ld[q])
                    if (s, j - 1) in exp_cnt:
                        c.wait("tensor", "AC", exp_cnt[(s, j - 1)])
                    base = kt_ap(s, j)
                    vp = None
                    for h_i in range(2):
                        r = 64 * h_i
                        vp = c.op("tensor", lambda e, s=s, j=j, r=r, base=base, i0=i0,
                                  iw=iw, ph=ph:
                                  e.matmul(PS[:, 1024 * s + 512 * (r // 64):
                                              1024 * s + 512 * (r // 64) + iw],
                                           SCR[r:r + 64, base:base + 128],
                                           A16[r:r + 64, 2112 * (2 * ph + s) + i0:
                                               2112 * (2 * ph + s) + i0 + iw],
                                           start=True, stop=True, tile_position=(r, 0)))
                    return vp

                def emit_exp(s, j):
                    slot = (j % PD) + PD * s
                    dep = slot_rd.get(slot)
                    if dep is not None:
                        c.wait("scalar", "PE", dep)
                    c.wait("scalar", "PE", qk_cnt[(s, j)])
                    if iw == 512:
                        va = c.op("scalar", lambda e, s=s, j=j, slot=slot:
                                  e.activation(out=SCR[:, 1024 * slot:1024 * slot + 1024],
                                               in_=PS[:, 1024 * s:1024 * s + 1024],
                                               func=AF.Exp,
                                               bias=g["gbC"][:, j:j + 1], scale=1.0))
                    else:
                        for h_i in range(2):
                            va = c.op("scalar", lambda e, s=s, j=j, slot=slot, h_i=h_i,
                                      iw=iw:
                                      e.activation(
                                          out=SCR[:, 1024 * slot + iw * h_i:
                                                  1024 * slot + iw * (h_i + 1)],
                                          in_=PS[:, 1024 * s + 512 * h_i:
                                                 1024 * s + 512 * h_i + iw],
                                          func=AF.Exp,
                                          bias=g["gbC"][:, j:j + 1], scale=1.0))
                    exp_cnt[(s, j)] = va[1]
                    return va

                def emit_pv(s, j):
                    slot = (j % PD) + PD * s
                    c.wait("tensor", "AC", exp_cnt[(s, j)])
                    vp = None
                    for h_i in range(2):
                        vp = c.op("tensor", lambda e, s=s, j=j, slot=slot, h_i=h_i, iw=iw:
                                  e.matmul(PS[64 * h_i:64 * h_i + 64, 512 * (4 + s):
                                              512 * (4 + s) + iw],
                                           SCR[:, 16384 + 256 * j + 128 * s + 64 * h_i:
                                               16384 + 256 * j + 128 * s + 64 * (h_i + 1)],
                                           SCR[:, 1024 * slot + iw * h_i:
                                               1024 * slot + iw * h_i + iw],
                                           start=False, stop=(j == NJ - 1),
                                           skip_group_check=True,
                                           tile_position=(0, 64 * h_i)))
                    return vp

                def emit_den(j):
                    vp = None
                    for hh in range(4):
                        s, h_i = hh // 2, hh % 2
                        slot = (j % PD) + PD * s
                        cpos = 32 * hh
                        vp = c.op("tensor", lambda e, j=j, s=s, h_i=h_i, slot=slot,
                                  cpos=cpos, iw=iw:
                                  e.matmul(PS[cpos:cpos + 1, 512 * 6:512 * 6 + iw],
                                           g["ones16"][:, 0:1],
                                           SCR[:, 1024 * slot + iw * h_i:
                                               1024 * slot + iw * h_i + iw],
                                           start=False, stop=(j == NJ - 1),
                                           skip_group_check=True,
                                           tile_position=(0, cpos)))
                    return vp

                qk_cnt = {}

                def do_qk(jq):
                    vp = emit_qk(0, jq)
                    qk_cnt[(0, jq)] = vp[1]
                    vp = emit_qk(1, jq)
                    qk_cnt[(1, jq)] = vp[1]
                    if jq % 16 == 15:
                        ktbuf_lastrd[(jq // 16) % 2] = ("PE", vp[1])
                        if jq // 16 + 2 <= 3:
                            load_quarter(jq // 16 + 2)
                    emit_exp(0, jq)
                    emit_exp(1, jq)

                do_qk(0)
                do_qk(1)
                if pending_fin[0] is not None:
                    pending_fin[0]()
                    pending_fin[0] = None
                do_qk(2)
                do_qk(3)
                for j in range(NJ):
                    if j == 0:
                        c.waitd("tensor", mset_dep[0])
                    emit_pv(0, j)
                    emit_pv(1, j)
                    dn = emit_den(j)
                    slot_rd[(j % PD)] = dn[1]
                    slot_rd[(j % PD) + PD] = dn[1]
                    if j + 4 < NJ:
                        do_qk(j + 4)

                c.waitd("vector", dn)
                c.wait("vector", "PE", c.cur("PE"))
                vrec = c.op("vector", lambda e, iw=iw: e.reciprocal(
                    out=g["recip_s"][:, 0:iw], in_=bank(6)[:, 0:iw]))
                c.waitd("gpsimd", vrec)
                d96 = c.dma("gpsimd", g["r96"][0:1, 0:iw],
                            g["recip_s"][96:97, 0:iw], sem="gd_r96")

                def make_fin(i0=i0, iw=iw, ph=ph, dn=dn, vrec=vrec, d96=d96):
                    def fin():
                        aux_cp = None
                        for s in range(2):
                            c.waitd("tensor", vrec)
                            if s == 1:
                                c.waitd("tensor", d96)
                                c.waitd("tensor", aux_cp)
                            srcs = [g["recip_s"][0:1, 0:iw] if s == 0 else
                                    g["recip_s"][64:65, 0:iw],
                                    g["recip_s"][32:33, 0:iw] if s == 0 else
                                    g["r96"][0:1, 0:iw]]
                            obase = [0 if s == 0 else 64, 32 if s == 0 else 0]
                            c.fence_pe()
                            for h_i in range(2):
                                vp = c.op("tensor", lambda e, s=s, h_i=h_i, iw=iw,
                                          srcs=srcs, ob2=obase[h_i]:
                                          e.matmul(PS[64 * h_i:64 * h_i + 64,
                                                      512 * 7:512 * 7 + iw],
                                                   g["onesf"][ob2:ob2 + 1, 0:64],
                                                   srcs[h_i], start=True, stop=True,
                                                   tile_position=(ob2, 64 * h_i)))
                            c.fence_pe()
                            c.waitd("vector", vp)
                            c.wait("vector", "VE", c.cur("VE"))
                            vv = c.op("vector", lambda e, iw=iw: e.tensor_copy(
                                out=g["bc_sb"][:, 0:iw], in_=bank(7)[:, 0:iw]))
                            aux_cp = vv
                            c.waitd("vector", vv)
                            vv2 = c.op("vector", lambda e, s=s, i0=i0, iw=iw, ph=ph:
                                       e.tensor_tensor(
                                           out=B16[:, 2112 * (2 * ph + s) + i0:
                                                   2112 * (2 * ph + s) + i0 + iw],
                                           in0=PS[:, 512 * (4 + s):512 * (4 + s) + iw],
                                           in1=g["bc_sb"][:, 0:iw], op=AL.mult))
                            aoT_dep[0] = vv2
                            c.waitd("vector", vv2)
                            vz = c.op("vector", lambda e, s=s: e.memset(
                                bank(4 + s)[:, :], 0.0))
                            mset_dep[0] = vz
                        c.waitd("vector", d96)
                        c.wait("vector", "VE", c.cur("VE"))
                        vz2 = c.op("vector", lambda e: e.memset(bank(6)[:, :], 0.0))
                        mset_dep[0] = vz2
                    return fin

                pending_fin[0] = make_fin()
            vph_rd[0] = ("PE", c.cur("PE"))
            if MKDBG == 2 and ph == int(os.environ.get("MK_NPH", "4")) - 1:
                c.waitd("gpsimd", aoT_dep[0])
                dd = c.dma("gpsimd", g["dbg1"][:, 0:4224 * (ph + 1)],
                           B16[:, 0:4224 * (ph + 1)], sem="gd_dbg")
                c.waitd("gpsimd", dd)
        if pending_fin[0] is not None:
            pending_fin[0]()
            pending_fin[0] = None
        return aoT_dep[0]

    def p4(ao_dep):
        c.wait("tensor", "VE", c.cur("VE"))
        c.wait("tensor", "AC", c.cur("AC"))
        wdep = load_w("wout", SCR, 0, 8, 1024, deps=(("PE", c.cur("PE")),))
        c.waitd("tensor", wdep)
        c.waitd("tensor", ao_dep)
        ps_rd = [None] * 4
        for k in range(NBLK):
            nr, t0 = blk_nr(k), blk_t0(k)
            for df in range(2):
                pb = (2 * k + df) % 4
                c.waitd("tensor", ps_rd[pb])
                vp = None
                npc = 2 * int(os.environ.get("MK_NPH", "4"))
                for pc in range(npc):
                    vp = c.op("tensor", lambda e, k=k, nr=nr, t0=t0, df=df, pc=pc, pb=pb,
                              npc=npc:
                              e.matmul(bank(pb)[0:nr, :],
                                       B16[:, 2112 * pc + t0:2112 * pc + t0 + nr],
                                       SCR[:, 1024 * pc + 512 * df:1024 * pc + 512 * (df + 1)],
                                       start=(pc == 0), stop=(pc == npc - 1)))
                c.waitd("vector", vp)
                vv = c.op("vector", lambda e, k=k, nr=nr, df=df, pb=pb: e.tensor_tensor(
                    out=x_sb[0:nr, 1024 * k + 512 * df:1024 * k + 512 * (df + 1)],
                    in0=x_sb[0:nr, 1024 * k + 512 * df:1024 * k + 512 * (df + 1)],
                    in1=bank(pb)[0:nr, :], op=AL.add))
                ps_rd[pb] = vv
        return ("VE", c.cur("VE"))

    def p5(p4_dep):
        c.wait("tensor", "VE", c.cur("VE"))
        c.wait("tensor", "AC", c.cur("AC"))
        c.waitd("vector", p4_dep)
        vv = c.op("vector", lambda e: e.tensor_scalar(
            out=x_sb[0:64, 0:1024], in0=x_sb[0:64, 0:1024],
            scalar1=g["hmask_s"][:, :], scalar2=None, op0=AL.mult))
        x2t_dep = [None] * NBLK
        tr_dep = [None] * NBLK

        def apply_cb(k, nr, sc, bi):
            col = 16896 + 1024 * (k % 4)
            if k >= 4:
                c.waitd("vector", tr_dep[k - 4])
            vv = c.op("vector", lambda e, k=k, nr=nr, col=col, sc=sc, bi=bi: e.tensor_scalar(
                out=SCR[0:nr, col:col + 1024], in0=x_sb[0:nr, 1024 * k:1024 * (k + 1)],
                scalar1=stats[0:nr, sc:sc + 1], scalar2=stats[0:nr, bi:bi + 1],
                op0=AL.mult, op1=AL.add))
            dst = SCR[:, 0:16896].rearrange("p (cc t) -> p cc t", cc=8)[
                :, :, blk_t0(k):blk_t0(k) + nr]
            tr_dep[k] = to_x1t(SCR[:, col:col + 1024], nr, dst, vv)
            x2t_dep[k] = tr_dep[k]

        ln_pass(NBLK, blk_nr, lambda k: x_sb[0:blk_nr(k), 1024 * k:1024 * (k + 1)],
                apply_cb, xdeps=(vv,))

        wdep_a = load_w("wka", A16, 0, 8, 1024, deps=(("PE", c.cur("PE")),))
        wdep = load_w("wkb", A16, 8192, 8, 1024)
        c.waitd("tensor", wdep_a)
        c.waitd("tensor", wdep)
        for k in range(NBLK):
            c.waitd("tensor", x2t_dep[k])
        ps_rd = [None] * 4
        kact_dep = None
        for dch in range(8):
            for tt in range(4):
                pb = (4 * dch + tt) % 4
                c.waitd("tensor", ps_rd[pb])
                vp = None
                for cc in range(8):
                    vp = c.op("tensor", lambda e, dch=dch, tt=tt, cc=cc, pb=pb:
                              e.matmul(bank(pb)[:, :],
                                       A16[:, 1024 * cc + 128 * dch:1024 * cc + 128 * (dch + 1)],
                                       X1T(cc, HALO + 512 * tt, 512),
                                       start=(cc == 0), stop=False))
                    vp = c.op("tensor", lambda e, dch=dch, tt=tt, cc=cc, pb=pb:
                              e.matmul(bank(pb)[:, :],
                                       A16[:, 8192 + 1024 * cc + 128 * dch:
                                           8192 + 1024 * cc + 128 * (dch + 1)],
                                       X1T(cc, HALO - 1 + 512 * tt, 512),
                                       start=False, stop=(cc == 7)))
                c.waitd("vector", vp)
                c.wait("vector", "VE", c.cur("VE"))
                vv = c.op("vector", lambda e, pb=pb: e.tensor_scalar_max(
                    out=g["bc_sb"][:, 0:512], in0=bank(pb)[:, :], scalar1=0.0))
                c.waitd("vector", vv)
                vv = c.op("vector", lambda e, dch=dch, tt=tt: e.tensor_tensor(
                    out=SCR[:, 16896 + 2048 * dch + 512 * tt:16896 + 2048 * dch + 512 * (tt + 1)],
                    in0=g["bc_sb"][:, 0:512], in1=g["bc_sb"][:, 0:512], op=AL.mult))
                ps_rd[pb] = vv
                kact_dep = vv

        wdep_a = load_w("wra", A16, 0, 8, 1024, deps=(("PE", c.cur("PE")),))
        wdep = load_w("wrb", A16, 8192, 8, 1024)
        wdep2 = load_w("wvf", B16, 0, 8, 1024)
        c.waitd("tensor", wdep_a)
        c.waitd("tensor", wdep)
        c.waitd("tensor", wdep2)
        c.waitd("tensor", kact_dep)
        rdep = {}
        vdep = {}
        for tc in range(16):
            pr = (tc % 2) * 4
            c.waitd("tensor", rdep.get(tc - 2))
            vp = None
            for df in range(2):
                for cc in range(8):
                    vp = c.op("tensor", lambda e, tc=tc, df=df, cc=cc, pr=pr:
                              e.matmul(bank(pr + df)[:, :],
                                       SCR[:, 2112 * cc + HALO + 128 * tc:
                                           2112 * cc + HALO + 128 * (tc + 1)],
                                       A16[:, 1024 * cc + 512 * df:1024 * cc + 512 * (df + 1)],
                                       start=(cc == 0), stop=False))
                    vp = c.op("tensor", lambda e, tc=tc, df=df, cc=cc, pr=pr:
                              e.matmul(bank(pr + df)[:, :],
                                       SCR[:, 2112 * cc + HALO - 1 + 128 * tc:
                                           2112 * cc + HALO - 1 + 128 * (tc + 1)],
                                       A16[:, 8192 + 1024 * cc + 512 * df:
                                           8192 + 1024 * cc + 512 * (df + 1)],
                                       start=False, stop=(cc == 7)))
            c.waitd("scalar", vp)
            rcol = 4096 + 1024 * (tc % 2)
            c.waitd("scalar", rdep.get(tc - 2))
            va = c.op("scalar", lambda e, pr=pr, rcol=rcol: e.activation(
                out=B16F[:, rcol:rcol + 1024], in_=bank(pr, 2)[:, 0:1024], func=AF.Sigmoid))
            rdep[tc] = va
            c.waitd("tensor", vdep.get(tc - 2))
            vp = None
            for df in range(2):
                for cc in range(8):
                    vp = c.op("tensor", lambda e, tc=tc, df=df, cc=cc, pr=pr:
                              e.matmul(bank(pr + 2 + df)[:, :],
                                       SCR[:, 16896 + 2048 * cc + 128 * tc:
                                           16896 + 2048 * cc + 128 * (tc + 1)],
                                       B16[:, 1024 * cc + 512 * df:1024 * cc + 512 * (df + 1)],
                                       start=(cc == 0), stop=(cc == 7)))
            c.waitd("vector", vp)
            c.waitd("vector", va)
            scol = 6144 + 1024 * (tc % 2)
            vv = c.op("vector", lambda e, pr=pr, rcol=rcol, scol=scol: e.scalar_tensor_tensor(
                out=B16F[:, scol:scol + 1024], in0=bank(pr + 2, 2)[:, 0:1024], scalar=1.0,
                in1=B16F[:, rcol:rcol + 1024], op0=AL.mult, op1=AL.mult))
            vdep[tc] = vv
            c.waitd("vector", vv)
            kb = 1 + tc
            vv = c.op("vector", lambda e, kb=kb, scol=scol: e.tensor_tensor(
                out=x_sb[:, 1024 * kb:1024 * (kb + 1)],
                in0=x_sb[:, 1024 * kb:1024 * (kb + 1)],
                in1=B16F[:, scol:scol + 1024], op=AL.add))
            vdep[tc] = vv
            rdep[tc] = vv
        return ("VE", c.cur("VE"))

    P1N = int(os.environ.get("MK_P1", "4"))
    p0_main()
    g_slotst = P0ST["g_slotst"]
    p1a()
    if P1N >= 2:
        p1b()
    if P1N >= 3:
        p1c()
    if P1N >= 4:
        xw_dep = p1d()
    else:
        xw_dep = ("VE", c.cur("VE"))
    if PHASES >= 2:
        p2_done = p2()
    if PHASES >= 3:
        ao_dep = p3(p2_done)
        if MKDBG == 1:
            c.waitd("gpsimd", ao_dep)
            dbgd = c.dma("gpsimd", g["dbg1"][:, 0:8448], B16[:, 0:8448], sem="gd_dbg")
            c.waitd("gpsimd", dbgd)
            dbgd = c.dma("gpsimd", g["dbg1"][:, 8448:16896], B16[:, 8448:16896],
                         sem="gd_dbg")
            c.waitd("gpsimd", dbgd)
        x4_dep = p4(ao_dep)
    else:
        x4_dep = xw_dep
    if PHASES >= 5:
        x5_dep = p5(x4_dep)
    else:
        x5_dep = x4_dep

    c.waitd("gpsimd", x5_dep)
    dst = c.dma("gpsimd", g["y"].rearrange("(k p) m -> p k m", p=128),
                x_sb[:, 1024:].rearrange("p (k m) -> p k m", k=16), sem="gd_y")
    c.waitd("gpsimd", dst)


_CACHED = {}


def kernel(L0_attn_x, L0_attn_w_v, L0_attn_b_v, L0_attn_w_wqk, L0_attn_b_wqk,
           L0_attn_w_qk, L0_attn_b_qk, L0_attn_w_out, L0_attn_b_out,
           L0_attn_w_pos_embedding, L0_attn_pos_embedding,
           L0_ffn_W_r, L0_ffn_W_k, L0_ffn_W_v, L0_ffn_mu_r, L0_ffn_mu_k):
    import ml_dtypes
    x = np.ascontiguousarray(L0_attn_x)
    pos = L0_attn_pos_embedding
    wpos = L0_attn_w_pos_embedding
    ident = np.eye(128).astype(ml_dtypes.bfloat16)
    wb = np.ascontiguousarray(wpos[WS - 1 - np.arange(WS)])
    wb128 = np.concatenate([wb, wb]).reshape(128, 1).astype(np.float32)

    nc = build_program()
    in_maps = []
    for core in range(8):
        b, qc = core // 4, core % 4
        start = 2048 * qc
        if qc == 0:
            halo = np.zeros((HALO, D), np.float32)
            hm = np.zeros((64, 1), np.float32)
        else:
            halo = x[b, start - HALO:start]
            hm = np.ones((64, 1), np.float32)
        x_ownh = np.ascontiguousarray(np.concatenate([halo, x[b, start:start + O]], axis=0))
        x_rest = np.ascontiguousarray(
            np.concatenate([x[b, :start], x[b, start + O:]], axis=0))
        perm = np.concatenate([np.arange(start, start + O),
                               np.arange(0, start),
                               np.arange(start + O, L)])
        gidx = np.clip(L - 1 - perm, 0, pos.shape[0] - 1)
        gbias = pos[gidx]
        gb_dev = np.ascontiguousarray(gbias.reshape(64, 128).T).astype(np.float32)
        in_maps.append({
            "x_ownh": x_ownh.astype(np.float32),
            "x_rest": x_rest.astype(np.float32),
            "w_v": L0_attn_w_v, "b_v": L0_attn_b_v,
            "w_wqk": L0_attn_w_wqk, "b_wqk": L0_attn_b_wqk,
            "w_qk": L0_attn_w_qk, "b_qk": L0_attn_b_qk,
            "w_out": L0_attn_w_out, "b_out": L0_attn_b_out,
            "f_wr": L0_ffn_W_r, "f_wk": L0_ffn_W_k, "f_wv": L0_ffn_W_v,
            "mu_r": L0_ffn_mu_r, "mu_k": L0_ffn_mu_k,
            "gb": gb_dev, "wbv": wb128, "hmask": hm, "ident": ident,
        })
    res = run_bass_kernel_spmd(nc, in_maps, core_ids=list(range(8)))
    out = np.empty((B, L, D), np.float32)
    for core in range(8):
        b, qc = core // 4, core % 4
        out[b, 2048 * qc:2048 * (qc + 1)] = res.results[core]["y"]
    return out


# version 12
# speedup vs baseline: 1.0209x; 1.0209x over previous
import os
import numpy as np
import concourse.bass as bass
import concourse.mybir as mybir
from concourse.bass_utils import run_bass_kernel_spmd

F32 = mybir.dt.float32
BF16 = mybir.dt.bfloat16
AF = mybir.ActivationFunctionType
AL = mybir.AluOpType
AX = mybir.AxisListType

B, L, D = 2, 8192, 1024
H, HD, WS = 16, 64, 64
O = 2048
HALO = 64
T = HALO + O
R = L - O
CSH = 20.0
NBLK = 17
PHASES = int(os.environ.get("MK_PHASES", "9"))
MKDBG = int(os.environ.get("MK_DEBUG", "0"))


def blk_nr(k):
    return 64 if k == 0 else 128


def blk_t0(k):
    return 0 if k == 0 else 64 + 128 * (k - 1)


class Ctx:
    def __init__(self, nc):
        self.nc = nc
        self.prog = {k: [] for k in ("sync", "vector", "tensor", "scalar", "gpsimd")}
        self.cnt = {}
        self.sem = {}
        self.pool = []

    def get_sem(self, name):
        if name not in self.sem:
            assert self.pool, "semaphore pool exhausted"
            self.sem[name] = self.pool.pop()
        return self.sem[name]

    def emit(self, eng, fn):
        self.prog[eng].append(fn)

    def bump(self, name, amt=1):
        self.cnt[name] = self.cnt.get(name, 0) + amt
        return self.cnt[name]

    def cur(self, name):
        return self.cnt.get(name, 0)

    def wait(self, eng, name, val):
        if val is None or val <= 0:
            return
        s = self.get_sem(name)
        self.emit(eng, lambda e, s=s, v=val: e.wait_ge(s, v))

    def waitd(self, eng, dep):
        if dep:
            self.wait(eng, dep[0], dep[1])

    def dma(self, queue, out, in_, sem="ld"):
        v = self.bump(sem, 16)
        s = self.get_sem(sem)
        self.emit(queue, lambda e, o=out, i=in_, s=s: e.dma_start(out=o, in_=i).then_inc(s, 16))
        return (sem, v)

    def op(self, eng, fn):
        sname = {"tensor": "PE", "vector": "VE", "scalar": "AC"}[eng]
        v = self.bump(sname)
        s = self.get_sem(sname)
        self.emit(eng, lambda e, f=fn, s=s: f(e).then_inc(s, 1))
        return (sname, v)

    def opq(self, eng, fn):
        self.emit(eng, fn)

    def fence_pe(self):
        self.wait("tensor", "PE", self.cur("PE"))


def build_program():
    nc = bass.Bass("TRN2", target_bir_lowering=False)
    g = {}
    for nm, shp, dt in (
        ("x_ownh", [T, D], F32), ("x_rest", [R, D], F32),
        ("w_v", [D, D], F32), ("b_v", [D], F32),
        ("w_wqk", [D, 2 * D], F32), ("b_wqk", [2 * D], F32),
        ("w_qk", [D, 2 * D], F32), ("b_qk", [2 * D], F32),
        ("w_out", [D, D], F32), ("b_out", [D], F32),
        ("f_wr", [D, D], F32), ("f_wk", [D, D], F32), ("f_wv", [D, D], F32),
        ("mu_r", [D], F32), ("mu_k", [D], F32),
        ("gb", [128, 64], F32), ("wbv", [128, 1], F32), ("hmask", [64, 1], F32),
        ("ident", [128, 128], BF16),
    ):
        g[nm] = nc.dram_tensor(nm, shp, dt, kind="ExternalInput").ap()
    g["y"] = nc.dram_tensor("y", [O, D], F32, kind="ExternalOutput").ap()
    if MKDBG:
        g["dbg1"] = nc.dram_tensor("dbg1", [128, 16896], BF16, kind="ExternalOutput").ap()

    g["kT_d"] = nc.dram_tensor("kT_d", [D, HALO + L], BF16).ap()
    g["qT_d"] = nc.dram_tensor("qT_d", [D, T], BF16).ap()
    g["v_d"] = nc.dram_tensor("v_d", [HALO + L, D], BF16).ap()
    g["w16"] = {}
    for nm, shp in (("wv", [D, D]), ("wwqk", [D, 2 * D]), ("wqk", [D, 2 * D]),
                    ("wout", [D, D]), ("wra", [D, D]), ("wrb", [D, D]),
                    ("wka", [D, D]), ("wkb", [D, D]), ("wvf", [D, D])):
        g["w16"][nm] = nc.dram_tensor("w16_" + nm, shp, BF16).ap()

    import contextlib
    with contextlib.ExitStack() as _st:
        en = _st.enter_context
        x_sb = en(nc.sbuf_tensor("x_sb", [128, NBLK * 1024], F32))
        A16 = en(nc.sbuf_tensor("A16", [128, 16896], BF16))
        B16 = en(nc.sbuf_tensor("B16", [128, 16896], BF16))
        SCR = en(nc.sbuf_tensor("SCR", [128, 33280], BF16))
        gbC = en(nc.sbuf_tensor("gbC", [128, 64], F32))
        wb1 = en(nc.sbuf_tensor("wb1", [128, 1], F32))
        hmask_s = en(nc.sbuf_tensor("hmask_s", [64, 1], F32))
        bqk_s = en(nc.sbuf_tensor("bqk_s", [128, 16], F32))
        bwqk_s = en(nc.sbuf_tensor("bwqk_s", [128, 16], F32))
        muT = en(nc.sbuf_tensor("muT", [128, 16], F32))
        ident16 = en(nc.sbuf_tensor("ident16", [128, 128], BF16))
        ones16 = en(nc.sbuf_tensor("ones16", [128, 8], BF16))
        onesf = en(nc.sbuf_tensor("onesf", [65, 128], F32))
        stats = en(nc.sbuf_tensor("stats", [128, 96], F32))
        r96 = en(nc.sbuf_tensor("r96", [1, 512], F32))
        stg16 = en(nc.sbuf_tensor("stg16", [128, 2048], BF16))
        recip_s = stg16.bitcast(F32)[:, 0:512]
        bc_sb = stg16.bitcast(F32)[:, 512:1024]
        PS = en(nc.psum_tensor("PS", [128, 4096], F32))
        sempool = [en(nc.semaphore(f"S{i}")) for i in range(40)]
        block = en(nc.Block())

        c = Ctx(nc)
        c.pool = sempool
        loc = locals()
        for nm in ("x_sb", "A16", "B16", "SCR", "gbC", "wb1", "hmask_s", "bqk_s",
                   "bwqk_s", "muT", "ident16", "ones16", "onesf", "stats",
                   "recip_s", "r96", "bc_sb", "stg16", "PS"):
            g[nm] = loc[nm]
        g["PS16"] = PS.bitcast(BF16)
        g["SCRF"] = SCR.bitcast(F32)
        g["A16F"] = A16.bitcast(F32)
        g["B16F"] = B16.bitcast(F32)

        build_all(c, g)

        @block.sync
        def _(e):
            for f in c.prog["sync"]:
                f(e)

        @block.vector
        def _(e):
            for f in c.prog["vector"]:
                f(e)

        @block.tensor
        def _(e):
            for f in c.prog["tensor"]:
                f(e)

        @block.scalar
        def _(e):
            for f in c.prog["scalar"]:
                f(e)

        @block.gpsimd
        def _(e):
            for f in c.prog["gpsimd"]:
                f(e)

    return nc


def build_all(c, g):
    x_sb, A16, B16, SCR = g["x_sb"], g["A16"], g["B16"], g["SCR"]
    PS, PS16, SCRF, B16F = g["PS"], g["PS16"], g["SCRF"], g["B16F"]
    stats, stg16 = g["stats"], g["stg16"]
    w16 = g["w16"]

    def bank(b, w=1):
        return PS[:, 512 * b:512 * (b + w)]

    def bank16(b, w=1):
        return PS16[:, 1024 * b:1024 * (b + w)]

    c.dma("sync", x_sb[0:64, 0:1024], g["x_ownh"][0:64, :], sem="ld_in")
    c.dma("sync", x_sb[:, 1024:].rearrange("p (k m) -> p k m", k=16),
          g["x_ownh"][64:T, :].rearrange("(k p) m -> p k m", p=128), sem="ld_in")
    c.dma("sync", g["gbC"][:, :], g["gb"], sem="ld_in")
    c.dma("sync", g["wb1"][:, :], g["wbv"], sem="ld_in")
    c.dma("sync", g["hmask_s"][:, :], g["hmask"], sem="ld_in")
    c.dma("sync", g["ident16"][:, :], g["ident"], sem="ld_in")
    d_in = ("ld_in", c.cur("ld_in"))
    d_xh = d_xo = d_cst = d_in

    c.waitd("vector", d_cst)
    v_stats0 = c.op("vector", lambda e: e.memset(g["stats"][:, :], 0.0))
    c.op("vector", lambda e: e.memset(g["ones16"][:, :], 1.0))
    c.op("vector", lambda e: e.memset(g["onesf"][:, :], 1.0))
    c.op("vector", lambda e: e.tensor_scalar_add(out=g["gbC"][:, :], in0=g["gbC"][:, :],
                                                 scalar1=-CSH))
    A16F, B16F = g["A16F"], g["B16F"]
    idf = B16F[:, 8192:8320]
    v_idf = c.op("vector", lambda e: e.tensor_copy(out=idf, in_=g["ident16"][:, :]))

    last_b7_rd = [None]

    def tr_small(dst, src_ap, n):
        stage = B16F[:, 8320:8448]
        dl = c.dma("sync", stage[0:n, 0:128], src_ap, sem="ld_st0")
        c.waitd("tensor", dl)
        c.waitd("tensor", v_idf)
        c.waitd("tensor", last_b7_rd[0])
        vp = c.op("tensor", lambda e, n=n: e.transpose(bank(7)[:, 0:n], stage[0:n, 0:128], idf[0:n, 0:n]))
        c.waitd("vector", vp)
        last_b7_rd[0] = c.op("vector", lambda e, d=dst, n=n: e.tensor_copy(
            out=d, in_=bank(7)[:, 0:n]))
        c.waitd("sync", vp)

    tr_small(g["bqk_s"][:, :], g["b_qk"].rearrange("(a p) -> a p", p=128), 16)
    tr_small(g["bwqk_s"][:, :], g["b_wqk"].rearrange("(a p) -> a p", p=128), 16)
    tr_small(g["muT"][:, 0:8], g["mu_r"].rearrange("(a p) -> a p", p=128), 8)
    tr_small(g["muT"][:, 8:16], g["mu_k"].rearrange("(a p) -> a p", p=128), 8)
    v_mu = last_b7_rd[0]

    bv_bc = SCRF[:, 14592:15616]
    bout_bc = SCRF[:, 15616:16640]

    def bias_bcast(dst, src_row):
        stage = SCRF[:, 11008:12032]
        c.wait("sync", "PE", c.cur("PE"))
        dl = c.dma("sync", stage[0:1, 0:1024], src_row.rearrange("(a m) -> a m", a=1),
                   sem="ld_bb")
        c.waitd("tensor", dl)
        for h in range(2):
            c.waitd("tensor", last_b7_rd[0])
            vp = c.op("tensor", lambda e, h=h: e.matmul(
                bank(7)[:, 0:512], g["onesf"][0:1, :], stage[0:1, 512 * h:512 * (h + 1)],
                start=True, stop=True))
            c.waitd("vector", vp)
            last_b7_rd[0] = c.op("vector", lambda e, h=h, d=dst: e.tensor_copy(
                out=d[:, 512 * h:512 * (h + 1)], in_=bank(7)[:, 0:512]))

    bias_bcast(bv_bc, g["b_v"])
    bias_bcast(bout_bc, g["b_out"])

    P0ST = {}

    def p0_main():
        stgf = [A16F[:, 2048 * i:2048 * (i + 1)] for i in range(4)]
        stcast = [None] * 4
        slot_cast = [None, None]
        slot_st = [None, None]
        wi = 0
        for nm, src, ncol in (("wv", g["w_v"], 1024), ("wwqk", g["w_wqk"], 2048),
                              ("wqk", g["w_qk"], 2048), ("wout", g["w_out"], 1024)):
            for ch in range(8):
                sl = wi % 4
                s = wi % 2
                wide = ncol == 2048
                q = "sync" if sl % 2 == 0 else "gpsimd"
                c.waitd(q, stcast[sl])
                dl = c.dma(q, stgf[sl][:, 0:ncol], src[128 * ch:128 * (ch + 1), :],
                           sem="ld_sg%d" % sl)
                c.waitd("vector", dl)
                c.waitd("vector", slot_st[s])
                if wide:
                    c.waitd("vector", slot_st[1 - s])
                vcast = c.op("vector", lambda e, sl=sl, s=s, n=ncol: e.tensor_copy(
                    out=(stg16[:, 1024 * s:1024 * s + n] if n == 1024 else stg16[:, 0:2048]),
                    in_=stgf[sl][:, 0:n]))
                stcast[sl] = vcast
                slot_cast[s] = vcast
                if wide:
                    slot_cast[1 - s] = vcast
                c.waitd("scalar", vcast)
                stv = c.dma("scalar", w16[nm][128 * ch:128 * (ch + 1), :],
                            (stg16[:, 1024 * s:1024 * s + ncol] if ncol == 1024 else
                             stg16[:, 0:2048]), sem="st_w%d" % s)
                slot_st[s] = stv
                if wide:
                    slot_st[1 - s] = stv
                wi += 1

        WT = [B16F[:, 1024 * i:1024 * (i + 1)] for i in range(8)]
        one_m = stats[:, 85:86]
        c.waitd("vector", v_mu)
        ld_stage_rd = {}
        for src, nma, nmb, mu_off in ((g["f_wr"], "wra", "wrb", 0),
                                      (g["f_wk"], "wka", "wkb", 8),
                                      (g["f_wv"], "wvf", None, None)):
            for dch in range(8):
                sl = dch % 4
                q = "sync" if sl % 2 == 0 else "gpsimd"
                c.waitd(q, ld_stage_rd.get(sl))
                c.waitd(q, stcast[sl])
                stcast[sl] = None
                dl = c.dma(q, stgf[sl][:, 0:1024], src[128 * dch:128 * (dch + 1), :],
                           sem="ld_sg%d" % sl)
                c.waitd("tensor", dl)
                for c4 in range(2):
                    c.waitd("tensor", last_b7_rd[0])
                    for cc in range(4 * c4, 4 * c4 + 4):
                        vp = c.op("tensor", lambda e, cc=cc, sl=sl, c4=c4: e.transpose(
                            bank(7)[:, 128 * (cc - 4 * c4):128 * (cc - 4 * c4) + 128],
                            stgf[sl][:, 128 * cc:128 * (cc + 1)], idf))
                    c.waitd("vector", vp)
                    dst4 = B16F[:, 0:8192].rearrange("p (cc m) -> p cc m", cc=8)[
                        :, 4 * c4:4 * c4 + 4, 128 * dch:128 * (dch + 1)]
                    last_b7_rd[0] = c.op("vector", lambda e, d=dst4: e.tensor_copy(
                        out=d, in_=bank(7).rearrange("p (cc m) -> p cc m", cc=4)))
                ld_stage_rd[sl] = vp
            c.waitd("vector", last_b7_rd[0])
            for cc in range(8):
                if nmb is not None:
                    c.waitd("vector", slot_st[0])
                    slot_cast[0] = c.op("vector", lambda e, cc=cc, o=mu_off: e.tensor_scalar(
                        out=stg16[:, 0:1024], in0=WT[cc], scalar1=g["muT"][:, o + cc:o + cc + 1],
                        scalar2=None, op0=AL.mult))
                    c.waitd("scalar", slot_cast[0])
                    slot_st[0] = c.dma("scalar", w16[nma][128 * cc:128 * (cc + 1), :],
                                       stg16[:, 0:1024], sem="st_w0")
                    c.waitd("vector", slot_cast[1])
                    vom = c.op("vector", lambda e, o=mu_off, cc=cc: e.tensor_scalar(
                        out=one_m, in0=g["muT"][:, o + cc:o + cc + 1], scalar1=-1.0, scalar2=1.0,
                        op0=AL.mult, op1=AL.add))
                    c.waitd("vector", vom)
                    c.waitd("vector", slot_st[1])
                    slot_cast[1] = c.op("vector", lambda e, cc=cc: e.tensor_scalar(
                        out=stg16[:, 1024:2048], in0=WT[cc], scalar1=one_m,
                        scalar2=None, op0=AL.mult))
                    c.waitd("scalar", slot_cast[1])
                    slot_st[1] = c.dma("scalar", w16[nmb][128 * cc:128 * (cc + 1), :],
                                       stg16[:, 1024:2048], sem="st_w1")
                else:
                    c.waitd("vector", slot_st[0])
                    slot_cast[0] = c.op("vector", lambda e, cc=cc: e.tensor_copy(
                        out=stg16[:, 0:1024], in_=WT[cc]))
                    c.waitd("scalar", slot_cast[0])
                    slot_st[0] = c.dma("scalar", w16["wvf"][128 * cc:128 * (cc + 1), :],
                                       stg16[:, 0:1024], sem="st_w0")

        p0_sd = [("st_w0", c.cur("st_w0")), ("st_w1", c.cur("st_w1"))]
        p0_ve = ("VE", c.cur("VE"))
        g_slotst = {0: slot_st[0], 1: slot_st[1]}
        P0ST.update(p0_sd=p0_sd, p0_ve=p0_ve, g_slotst=g_slotst)


    def X1T(cc, t0, tw):
        return SCR[:, 2112 * cc + t0:2112 * cc + t0 + tw]

    def ln_pass(nblks, nr_fn, src_fn, apply_cb, xdeps=(), junk_col=20992):
        first = True
        for k in range(nblks):
            nr = nr_fn(k)
            xb = src_fn(k)
            if first:
                for dpp in xdeps:
                    c.waitd("scalar", dpp)
                    c.waitd("vector", dpp)
                first = False
            if k > 0:
                c.wait("scalar", "AC", c.cur("AC"))
            c.op("scalar", lambda e, xb=xb, k=k, nr=nr: e.activation(
                out=SCR[0:nr, junk_col:junk_col + 1024], in_=xb, func=AF.Square,
                accum_out=stats[0:nr, 17 + k:18 + k]))
            c.op("vector", lambda e, xb=xb, k=k, nr=nr: e.reduce_sum(
                out=stats[0:nr, k:k + 1], in_=xb, axis=AX.X))
        c.wait("vector", "AC", c.cur("AC"))
        c.wait("vector", "VE", c.cur("VE"))
        n = nblks

        def vop(fn):
            c.wait("vector", "VE", c.cur("VE"))
            return c.op("vector", fn)

        vop(lambda e, n=n: e.tensor_scalar(
            out=stats[:, 34:34 + n], in0=stats[:, 0:n], scalar1=1.0 / D, scalar2=None,
            op0=AL.mult))
        vop(lambda e, n=n: e.tensor_scalar(
            out=stats[:, 51:51 + n], in0=stats[:, 17:17 + n], scalar1=1.0 / D, scalar2=None,
            op0=AL.mult))
        vop(lambda e, n=n: e.tensor_tensor(
            out=stats[:, 68:68 + n], in0=stats[:, 34:34 + n], in1=stats[:, 34:34 + n],
            op=AL.mult))
        vop(lambda e, n=n: e.tensor_tensor(
            out=stats[:, 51:51 + n], in0=stats[:, 51:51 + n], in1=stats[:, 68:68 + n],
            op=AL.subtract))
        vop(lambda e, n=n: e.tensor_scalar_add(
            out=stats[:, 51:51 + n], in0=stats[:, 51:51 + n], scalar1=1e-5))
        vv = vop(lambda e, n=n: e.reciprocal(
            out=stats[:, 68:68 + n], in_=stats[:, 51:51 + n]))
        c.waitd("scalar", vv)
        va = c.op("scalar", lambda e, n=n: e.activation(
            out=stats[:, 51:51 + n], in_=stats[:, 68:68 + n], func=AF.Sqrt))
        c.waitd("vector", va)
        vop(lambda e, n=n: e.scalar_tensor_tensor(
            out=stats[:, 68:68 + n], in0=stats[:, 34:34 + n], scalar=-1.0,
            in1=stats[:, 51:51 + n], op0=AL.mult, op1=AL.mult))
        c.wait("vector", "VE", c.cur("VE"))
        for k in range(nblks):
            apply_cb(k, nr_fn(k), 51 + k, 68 + k)

    def to_x1t(src_ap, nr, dst3_ap, dep):
        c.waitd("tensor", dep)
        c.waitd("tensor", last_b7_rd[0])
        vp = None
        for cc in range(8):
            vp = c.op("tensor", lambda e, cc=cc, nr=nr, a=src_ap: e.transpose(
                bank16(7)[:, 128 * cc:128 * cc + nr],
                a[0:nr, 128 * cc:128 * (cc + 1)], g["ident16"][0:nr, 0:nr]))
        c.waitd("vector", vp)
        vv = c.op("vector", lambda e, o=dst3_ap, nr=nr: e.tensor_copy(
            out=o, in_=bank16(7).rearrange("p (cc t) -> p cc t", cc=8)[:, :, 0:nr]))
        last_b7_rd[0] = vv
        return vv

    _lwn = [0]

    def load_w(nm, arena, col0, nch, ncol, src_col0=0, src_ncol=None, deps=()):
        for dpp in deps:
            c.waitd("sync", dpp)
        _lwn[0] += 1
        sem = "ld_w%d" % _lwn[0]
        dl = None
        if src_ncol is None:
            src_ncol = ncol
        for ch in range(nch):
            dl = c.dma("sync", arena[:, col0 + ncol * ch:col0 + ncol * ch + src_ncol],
                       w16[nm][128 * ch:128 * (ch + 1), src_col0:src_col0 + src_ncol],
                       sem=sem)
        return dl

    x1t_dep = [None] * NBLK

    def p1a():
        tr_dep = [None] * NBLK

        def apply_cb(k, nr, sc, bi):
            col = 16896 + 1024 * (k % 4)
            if k >= 4:
                c.waitd("vector", tr_dep[k - 4])
            vv = c.op("vector", lambda e, k=k, nr=nr, col=col, sc=sc, bi=bi: e.tensor_scalar(
                out=SCR[0:nr, col:col + 1024], in0=x_sb[0:nr, 1024 * k:1024 * (k + 1)],
                scalar1=stats[0:nr, sc:sc + 1], scalar2=stats[0:nr, bi:bi + 1],
                op0=AL.mult, op1=AL.add))
            dst = SCR[:, 0:16896].rearrange("p (cc t) -> p cc t", cc=8)[
                :, :, blk_t0(k):blk_t0(k) + nr]
            tr_dep[k] = to_x1t(SCR[:, col:col + 1024], nr, dst, vv)
            x1t_dep[k] = tr_dep[k]

        ln_pass(NBLK, blk_nr, lambda k: x_sb[0:blk_nr(k), 1024 * k:1024 * (k + 1)],
                apply_cb, xdeps=(d_xh, d_xo, v_stats0))

    def p1b():
        c.wait("tensor", "VE", c.cur("VE"))
        c.wait("tensor", "AC", c.cur("AC"))
        wdep = load_w("wv", B16, 0, 8, 1024, deps=tuple(P0ST["p0_sd"]))
        c.waitd("tensor", wdep)
        for k in range(NBLK):
            c.waitd("tensor", x1t_dep[k])
        ps_rd = [None, None]
        st_dep = {-2: g_slotst[0], -1: g_slotst[1]}
        for k in range(NBLK):
            nr, t0 = blk_nr(k), blk_t0(k)
            pb = (k % 2) * 2
            c.waitd("tensor", ps_rd[k % 2])
            vp = None
            for dh in range(2):
                for cc in range(8):
                    vp = c.op("tensor", lambda e, nr=nr, t0=t0, pb=pb, dh=dh, cc=cc:
                              e.matmul(bank(pb + dh)[0:nr, :], X1T(cc, t0, nr),
                                       B16[:, 1024 * cc + 512 * dh:1024 * cc + 512 * (dh + 1)],
                                       start=(cc == 0), stop=(cc == 7)))
            c.waitd("vector", vp)
            s = k % 2
            c.waitd("vector", st_dep.get(k - 2))
            vv = c.op("vector", lambda e, nr=nr, pb=pb, s=s: e.tensor_tensor(
                out=stg16[0:nr, 1024 * s:1024 * (s + 1)], in0=bank(pb, 2)[0:nr, 0:1024],
                in1=bv_bc[0:nr, :], op=AL.add))
            ps_rd[k % 2] = vv
            q = "scalar" if s == 0 else "gpsimd"
            c.waitd(q, vv)
            st_dep[k] = c.dma(q, g["v_d"][t0:t0 + nr, :],
                              stg16[0:nr, 1024 * s:1024 * (s + 1)],
                              sem=("st_w0" if s == 0 else "sg_w1"))
            g_slotst[s] = st_dep[k]

    def p1c():
        c.wait("tensor", "VE", c.cur("VE"))
        c.wait("tensor", "AC", c.cur("AC"))
        wdep = load_w("wqk", B16, 0, 8, 2048, deps=(("PE", c.cur("PE")),))
        c.waitd("tensor", wdep)
        ttiles = [(0, 512), (512, 512), (1024, 512), (1536, 512), (2048, 64)]
        ps_rd = [None] * 4
        st_dep = {-4: g_slotst[0], -3: g_slotst[1], -2: g_slotst[0], -1: g_slotst[1]}
        c.wait("vector", "st_w0", c.cur("st_w0"))
        c.wait("vector", "st_w1", c.cur("st_w1"))
        i = 0
        for dch in range(16):
            for (t0, tw) in ttiles:
                pb = i % 4
                c.waitd("tensor", ps_rd[pb])
                vp = None
                for cc in range(8):
                    vp = c.op("tensor", lambda e, dch=dch, t0=t0, tw=tw, pb=pb, cc=cc:
                              e.matmul(bank(pb)[:, 0:tw],
                                       B16[:, 2048 * cc + 128 * dch:2048 * cc + 128 * (dch + 1)],
                                       X1T(cc, t0, tw), start=(cc == 0), stop=(cc == 7)))
                c.waitd("vector", vp)
                s = i % 4
                c.waitd("vector", st_dep.get(i - 4))
                vv = c.op("vector", lambda e, dch=dch, tw=tw, pb=pb, s=s: e.tensor_scalar(
                    out=stg16[:, 512 * s:512 * s + tw], in0=bank(pb)[:, 0:tw],
                    scalar1=g["bqk_s"][:, dch:dch + 1], scalar2=None, op0=AL.add))
                ps_rd[pb] = vv
                q = "scalar" if s % 2 == 0 else "gpsimd"
                c.waitd(q, vv)
                if dch < 8:
                    dst = g["qT_d"][128 * dch:128 * (dch + 1), t0:t0 + tw]
                else:
                    dst = g["kT_d"][128 * (dch - 8):128 * (dch - 7), t0:t0 + tw]
                st_dep[i] = c.dma(q, dst, stg16[:, 512 * s:512 * s + tw],
                                  sem=("st_w%d" % s if s % 2 == 0 else "sg_w%d" % s))
                g_slotst[s % 2] = st_dep[i]
                i += 1

    def p1d():
        c.wait("tensor", "VE", c.cur("VE"))
        c.wait("tensor", "AC", c.cur("AC"))
        c.wait("vector", "st_w0", c.cur("st_w0"))
        c.wait("vector", "st_w1", c.cur("st_w1"))
        wdep = load_w("wwqk", B16, 0, 8, 2048, deps=(("PE", c.cur("PE")),))
        wodep = load_w("wout", A16, 0, 8, 1024, deps=(P0ST["p0_ve"],))
        c.waitd("sync", g_slotst[0])
        c.waitd("sync", g_slotst[1])
        wtiles = [(64 + 256 * i, 256) for i in range(8)] + [(0, 64)]
        if os.environ.get("MK_NOHALO"):
            wtiles = wtiles[:-1]
        arena_rd = [None]
        pw_rd = [None, None]
        s_exp = [None, None]
        wn_rd = [None, None]
        wtn_rd = [None]
        oproj_rd = [None, None]
        for (t0, tw) in wtiles:
            nwin = tw // 64
            c.waitd("sync", arena_rd[0])
            if tw == 256:
                dv = c.dma("sync", A16[:, 8192:10240].rearrange("p (s m) -> p s m", m=1024),
                           g["v_d"][t0:t0 + tw, :].rearrange("(s p) m -> p s m", p=128),
                           sem="ld_vt")
            else:
                dv = c.dma("sync", A16[0:64, 8192:9216], g["v_d"][t0:t0 + 64, :],
                           sem="ld_vt")
            ps_rd = [oproj_rd[0], oproj_rd[1]]
            wq_done = None
            for dch in range(16):
                pb = dch % 2
                c.waitd("tensor", ps_rd[pb])
                if dch == 0:
                    c.waitd("tensor", wdep)
                    c.waitd("tensor", arena_rd[0])
                vp = None
                for cc in range(8):
                    vp = c.op("tensor", lambda e, dch=dch, t0=t0, tw=tw, pb=pb, cc=cc:
                              e.matmul(bank(pb)[:, 0:tw],
                                       B16[:, 2048 * cc + 128 * dch:2048 * cc + 128 * (dch + 1)],
                                       X1T(cc, t0, tw), start=(cc == 0), stop=(cc == 7)))
                c.waitd("vector", vp)
                vv = c.op("vector", lambda e, dch=dch, tw=tw, pb=pb: e.tensor_scalar(
                    out=A16[:, 12288 + 256 * dch:12288 + 256 * dch + tw], in0=bank(pb)[:, 0:tw],
                    scalar1=g["bwqk_s"][:, dch:dch + 1], scalar2=None, op0=AL.add))
                ps_rd[pb] = vv
                wq_done = vv
            c.waitd("tensor", wq_done)
            c.waitd("tensor", dv)

            P1D = int(os.environ.get("MK_P1D", "5"))
            if P1D < 2:
                arena_rd[0] = wq_done
                continue
            vz = c.op("vector", lambda e: e.memset(bank(2, 4)[:, :], 0.0))
            vz2 = c.op("vector", lambda e: e.memset(bank(6)[:, 0:32], 0.0))
            c.waitd("tensor", vz)
            c.waitd("tensor", vz2)

            def qcol(hp):
                return 12288 + 256 * hp

            def kcol(hp):
                return 12288 + 256 * (8 + hp)

            if nwin == 4:
                last_pv = None
                for g2 in range(4):
                    sb = g2 % 2
                    c.waitd("tensor", s_exp[sb])
                    vp = None
                    for h_i in range(2):
                        for hp_i, hp in enumerate((2 * g2, 2 * g2 + 1)):
                            h = 2 * hp + h_i
                            r = 64 * h_i
                            for wp in range(2):
                                slot = 256 * hp_i + 128 * h_i + 64 * wp
                                for w_i, w in enumerate((2 * wp, 2 * wp + 1)):
                                    cc2 = 64 * w_i
                                    vp = c.op("tensor", lambda e, r=r, cc2=cc2, w=w, sb=sb,
                                              slot=slot, hp=hp, h_i=h_i:
                                              e.matmul(
                                                  bank(sb)[cc2:cc2 + 64, slot:slot + 64],
                                                  A16[r:r + 64, kcol(hp) + 64 * w:
                                                      kcol(hp) + 64 * (w + 1)],
                                                  A16[r:r + 64, qcol(hp) + 64 * w:
                                                      qcol(hp) + 64 * (w + 1)],
                                                  start=True, stop=True,
                                                  tile_position=(r, cc2)))
                        c.fence_pe()
                    if P1D < 3:
                        continue
                    c.waitd("scalar", vp)
                    pwc = 16896 + 512 * (g2 % 2)
                    c.waitd("scalar", pw_rd[g2 % 2])
                    va = c.op("scalar", lambda e, sb=sb, pwc=pwc: e.activation(
                        out=SCR[:, pwc:pwc + 512], in_=bank(sb)[:, 0:512], func=AF.Exp,
                        bias=g["wb1"][:, :], scale=0.125))
                    s_exp[sb] = va
                    c.waitd("tensor", va)
                    for hp_i, hp in enumerate((2 * g2, 2 * g2 + 1)):
                        for h_i in range(2):
                            h = 2 * hp + h_i
                            for wp in range(2):
                                slot = 256 * hp_i + 128 * h_i + 64 * wp
                                ob = 2 + 2 * wp
                                for w_i, w in enumerate((2 * wp, 2 * wp + 1)):
                                    r = 64 * w_i
                                    vbase = 8192 + 1024 * (w // 2)
                                    vp = c.op("tensor", lambda e, r=r, w=w, h=h, slot=slot,
                                              pwc=pwc, ob=ob, vbase=vbase:
                                              e.matmul(
                                                  PS[r:r + 64,
                                                     512 * (ob + (h >= 8)) + (64 * h) % 512:
                                                     512 * (ob + (h >= 8)) + (64 * h) % 512 + 64],
                                                  SCR[r:r + 64, pwc + slot:pwc + slot + 64],
                                                  A16[r:r + 64, vbase + 64 * h:
                                                      vbase + 64 * (h + 1)],
                                                  start=False, stop=True,
                                                  skip_group_check=True,
                                                  tile_position=(r, r)))
                                    vp2 = c.op("tensor", lambda e, r=r, h=h, slot=slot,
                                               pwc=pwc, wp=wp:
                                               e.matmul(
                                                   PS[r:r + 64, 512 * 6 + 16 * wp + h:
                                                      512 * 6 + 16 * wp + h + 1],
                                                   SCR[r:r + 64, pwc + slot:pwc + slot + 64],
                                                   g["ones16"][r:r + 64, 0:1],
                                                   start=False, stop=True,
                                                   skip_group_check=True,
                                                   tile_position=(r, r)))
                                    last_pv = vp2
                    pw_rd[g2 % 2] = last_pv
                    c.fence_pe()
                pv_all = last_pv
            else:
                HSTG = int(os.environ.get("MK_HALOSTAGE", "5"))
                if HSTG < 2:
                    arena_rd[0] = wq_done
                    continue
                vp = None
                c.waitd("tensor", s_exp[0])
                for h in range(16):
                    hp, h_i = h // 2, h % 2
                    r = 64 * h_i
                    ch = 512 * (h % 2) + 64 * (h // 2)
                    vp = c.op("tensor", lambda e, h=h, hp=hp, r=r, ch=ch: e.matmul(
                        bank(0, 2)[0:64, ch:ch + 64],
                        A16[r:r + 64, kcol(hp):kcol(hp) + 64],
                        A16[r:r + 64, qcol(hp):qcol(hp) + 64],
                        start=True, stop=True, tile_position=(r, 0)))
                if HSTG < 3:
                    arena_rd[0] = vp
                    continue
                c.waitd("scalar", vp)
                c.waitd("scalar", pw_rd[0])
                va = c.op("scalar", lambda e: e.activation(
                    out=SCR[0:64, 16896:17920], in_=bank(0, 2)[0:64, 0:1024], func=AF.Exp,
                    bias=g["wb1"][0:64, :], scale=0.125))
                s_exp[0] = va
                s_exp[1] = va
                if HSTG == 3 and False:
                    pass
                c.waitd("tensor", va)
                for h in range(16):
                    ob = 2 + (h >= 8)
                    ch = 512 * (h % 2) + 64 * (h // 2)
                    vp = c.op("tensor", lambda e, h=h, ob=ob, ch=ch: e.matmul(
                        PS[0:64, 512 * ob + (64 * h) % 512:512 * ob + (64 * h) % 512 + 64],
                        SCR[0:64, 16896 + ch:16896 + ch + 64],
                        A16[0:64, 8192 + 64 * h:8192 + 64 * (h + 1)],
                        start=False, stop=True, skip_group_check=True,
                        tile_position=(0, 0)))
                    vp2 = c.op("tensor", lambda e, h=h, ch=ch: e.matmul(
                        PS[0:64, 512 * 6 + h:512 * 6 + h + 1],
                        SCR[0:64, 16896 + ch:16896 + ch + 64],
                        g["ones16"][0:64, 0:1],
                        start=False, stop=True, skip_group_check=True,
                        tile_position=(0, 0)))
                pw_rd[0] = vp2
                pw_rd[1] = vp2
                pv_all = vp2
                if HSTG < 4:
                    arena_rd[0] = pv_all
                    continue

            if P1D < 3:
                arena_rd[0] = vp
                continue
            if P1D < 4:
                arena_rd[0] = pv_all
                continue
            c.waitd("vector", pv_all)
            nwp = max(1, nwin // 2)
            vrec = c.op("vector", lambda e: e.reciprocal(
                out=g["recip_s"][:, 0:32], in_=bank(6)[:, 0:32]))
            c.waitd("vector", vrec)
            tdep = {}
            for wp in range(nwp):
                wnc = 17920 + 1024 * wp
                nq = 128 if nwin == 4 else 64
                c.waitd("vector", wn_rd[wp])
                rbc = g["recip_s"][:, 16 * wp:16 * wp + 16].unsqueeze(2).broadcast_to(
                    (128, 16, 64))
                vv = c.op("vector", lambda e, wp=wp, wnc=wnc, nq=nq, rbc=rbc: e.tensor_tensor(
                    out=SCR[0:nq, wnc:wnc + 1024].rearrange("p (h d) -> p h d", h=16),
                    in0=bank(2 + 2 * wp, 2)[0:nq, 0:1024].rearrange("p (h d) -> p h d", h=16),
                    in1=rbc[0:nq], op=AL.mult))
                dst = SCR[:, 20992:23040].rearrange("p (cc t) -> p cc t", cc=8)[
                    :, :, 128 * wp:128 * wp + nq]
                tdep[wp] = to_x1t(SCR[:, wnc:wnc + 1024], nq, dst, vv)
                wn_rd[wp] = tdep[wp]
            if P1D < 5:
                arena_rd[0] = pv_all
                continue
            c.waitd("tensor", tdep[0])
            if nwp > 1:
                c.waitd("tensor", tdep[nwp - 1])
            c.waitd("tensor", wodep)
            nsub = 2 if nwin == 4 else 1
            for ts in range(nsub):
                nq = 128 if nwin == 4 else 64
                for df in range(2):
                    pb = df
                    c.waitd("tensor", oproj_rd[pb])
                    vp = None
                    for cc in range(8):
                        vp = c.op("tensor", lambda e, ts=ts, df=df, cc=cc, nq=nq, pb=pb:
                                  e.matmul(bank(pb)[0:nq, :],
                                           SCR[:, 20992 + 256 * cc + 128 * ts:
                                               20992 + 256 * cc + 128 * ts + nq],
                                           A16[:, 1024 * cc + 512 * df:
                                               1024 * cc + 512 * (df + 1)],
                                           start=(cc == 0), stop=(cc == 7)))
                    c.waitd("vector", vp)
                    c.wait("vector", "VE", c.cur("VE"))
                    tt = t0 + 128 * ts
                    kblk = 0 if tt < 64 else 1 + (tt - 64) // 128
                    vv = c.op("vector", lambda e, pb=pb, df=df, nq=nq: e.tensor_tensor(
                        out=g["bc_sb"][0:nq, :], in0=bank(pb)[0:nq, :],
                        in1=bout_bc[0:nq, 512 * df:512 * (df + 1)], op=AL.add))
                    oproj_rd[pb] = vv
                    c.waitd("vector", vv)
                    vv = c.op("vector", lambda e, kblk=kblk, df=df, nq=nq: e.tensor_tensor(
                        out=x_sb[0:nq, 1024 * kblk + 512 * df:1024 * kblk + 512 * (df + 1)],
                        in0=x_sb[0:nq, 1024 * kblk + 512 * df:1024 * kblk + 512 * (df + 1)],
                        in1=g["bc_sb"][0:nq, :], op=AL.add))
                    wtn_rd[0] = vp
            arena_rd[0] = vp
            s_exp[0] = oproj_rd[0]
            s_exp[1] = oproj_rd[1]
        p1d_x = ("VE", c.cur("VE"))
        return p1d_x

    def p2():
        c.wait("tensor", "VE", c.cur("VE"))
        c.wait("tensor", "AC", c.cur("AC"))
        pe_guard = ("PE", c.cur("PE"))
        wdep = load_w("wqk", B16, 0, 8, 1024, src_col0=1024, src_ncol=1024,
                      deps=(pe_guard,))
        wdep2 = load_w("wv", B16, 8192, 8, 1024)
        xst = SCRF[:, 4096:12288]

        def xblk(bk):
            return xst[:, 2048 * bk:2048 * bk + 1024]

        prev_use = [None]
        st_dep = {-2: g_slotst[0], -1: g_slotst[1]}
        ps_rd_v = [None, None]
        i = [0]
        for rt in range(12):
            c.waitd("sync", prev_use[0])
            c.wait("sync", "AC", c.cur("AC"))
            dx = c.dma("sync", xst.rearrange("p (bk m) -> p bk m", bk=4)[:, :, 0:1024],
                       g["x_rest"][512 * rt:512 * (rt + 1), :].rearrange(
                           "(bk p) m -> p bk m", p=128), sem="ld_xt")
            tr_dep = {}

            def apply_cb(k, nr, sc, bi, rt=rt):
                col = 24576 + 1024 * k
                vv = c.op("vector", lambda e, k=k, col=col, sc=sc, bi=bi: e.tensor_scalar(
                    out=SCR[:, col:col + 1024], in0=xblk(k),
                    scalar1=stats[:, sc:sc + 1], scalar2=stats[:, bi:bi + 1],
                    op0=AL.mult, op1=AL.add))
                dst = SCR[:, 0:4096].rearrange("p (cc t) -> p cc t", cc=8)[
                    :, :, 128 * k:128 * (k + 1)]
                tr_dep[k] = to_x1t(SCR[:, col:col + 1024], 128, dst, vv)

            ln_pass(4, lambda k: 128, xblk, apply_cb,
                    xdeps=(dx, ("PE", c.cur("PE"))), junk_col=28672 - 1024)
            c.waitd("tensor", tr_dep[3])
            prev_use[0] = tr_dep[3]

            def X1R(cc, tw=512):
                return SCR[:, 512 * cc:512 * cc + tw]

            ps_rd = [None, None]
            for dch in range(8):
                pb = i[0] % 2
                c.waitd("tensor", ps_rd[pb])
                if rt == 0 and dch == 0:
                    c.waitd("tensor", wdep)
                    c.waitd("tensor", wdep2)
                vp = None
                for cc in range(8):
                    vp = c.op("tensor", lambda e, dch=dch, pb=pb, cc=cc:
                              e.matmul(bank(pb)[:, 0:512],
                                       B16[:, 1024 * cc + 128 * dch:1024 * cc + 128 * (dch + 1)],
                                       X1R(cc), start=(cc == 0), stop=(cc == 7)))
                c.waitd("vector", vp)
                s = i[0] % 2
                c.waitd("vector", st_dep.get(i[0] - 2))
                vv = c.op("vector", lambda e, dch=dch, pb=pb, s=s: e.tensor_scalar(
                    out=stg16[:, 1024 * s:1024 * s + 512], in0=bank(pb)[:, 0:512],
                    scalar1=g["bqk_s"][:, 8 + dch:9 + dch], scalar2=None, op0=AL.add))
                ps_rd[pb] = vv
                q = "scalar" if s == 0 else "gpsimd"
                c.waitd(q, vv)
                st_dep[i[0]] = c.dma(
                    q, g["kT_d"][128 * dch:128 * (dch + 1),
                                 T + 512 * rt:T + 512 * (rt + 1)],
                    stg16[:, 1024 * s:1024 * s + 512],
                    sem=("st_w0" if s == 0 else "sg_w1"))
                g_slotst[s] = st_dep[i[0]]
                i[0] += 1
            for bk in range(4):
                pb2 = 2 + (bk % 2) * 2
                c.waitd("tensor", ps_rd_v[bk % 2])
                vp = None
                for dh in range(2):
                    for cc in range(8):
                        vp = c.op("tensor", lambda e, bk=bk, pb2=pb2, dh=dh, cc=cc:
                                  e.matmul(bank(pb2 + dh)[:, :],
                                           SCR[:, 512 * cc + 128 * bk:512 * cc + 128 * (bk + 1)],
                                           B16[:, 8192 + 1024 * cc + 512 * dh:
                                               8192 + 1024 * cc + 512 * (dh + 1)],
                                           start=(cc == 0), stop=(cc == 7)))
                c.waitd("vector", vp)
                s = i[0] % 2
                c.waitd("vector", st_dep.get(i[0] - 2))
                vv = c.op("vector", lambda e, pb2=pb2, s=s: e.tensor_tensor(
                    out=stg16[:, 1024 * s:1024 * (s + 1)], in0=bank(pb2, 2)[:, 0:1024],
                    in1=bv_bc[:, :], op=AL.add))
                ps_rd_v[bk % 2] = vv
                q = "scalar" if s == 0 else "gpsimd"
                c.waitd(q, vv)
                st_dep[i[0]] = c.dma(
                    q, g["v_d"][T + 512 * rt + 128 * bk:T + 512 * rt + 128 * (bk + 1), :],
                    stg16[:, 1024 * s:1024 * (s + 1)],
                    sem=("st_w0" if s == 0 else "sg_w1"))
                g_slotst[s] = st_dep[i[0]]
                i[0] += 1
        return ([("st_w%d" % sw, c.cur("st_w%d" % sw)) for sw in range(4)] +
                [(nm, c.cur(nm)) for nm in ("sg_w1", "sg_w3")])

    def p3(p2_done):
        c.wait("tensor", "VE", c.cur("VE"))
        c.wait("tensor", "AC", c.cur("AC"))
        qdep = None
        c.wait("sync", "PE", c.cur("PE"))
        for ch in range(8):
            qdep = c.dma("sync", A16[:, 2112 * ch:2112 * (ch + 1)],
                         g["qT_d"][128 * ch:128 * (ch + 1), :], sem="ld_qt")
        for dpp in p2_done:
            c.waitd("sync", dpp)
            c.waitd("gpsimd", dpp)
        ichunks = [(0, 512), (512, 512), (1024, 512), (1536, 512), (2048, 64)]
        NJ = 64
        PD = 4
        aoT_dep = [None]
        mset_dep = [None]
        pending_fin = [None]
        vph_rd = [None]
        ktbuf_lastrd = [None, None]
        slot_rd = {}
        exp_cnt = {}

        NPH = int(os.environ.get("MK_NPH", "4"))
        for ph in range(NPH):
            c.waitd("gpsimd", vph_rd[0])
            vph_ld = []
            for s8 in range(8):
                vph_ld.append(c.dma("gpsimd",
                              SCR[:, 16384 + 2048 * s8:16384 + 2048 * (s8 + 1)].rearrange(
                                  "p (j m) -> p j m", j=8),
                              g["v_d"][HALO + 1024 * s8:HALO + 1024 * (s8 + 1),
                                       256 * ph:256 * (ph + 1)].rearrange(
                                  "(j p) m -> p j m", p=128), sem="ld_sg%d" % (s8 // 2)))
            vdep = vph_ld[-1]
            if ph == 0:
                c.waitd("tensor", qdep)
                c.wait("vector", "VE", c.cur("VE"))
                vz = c.op("vector", lambda e: e.memset(bank(4, 2)[:, :], 0.0))
                vz2 = c.op("vector", lambda e: e.memset(bank(6)[:, :], 0.0))
                mset_dep[0] = vz2
                c.waitd("tensor", vz)
                c.waitd("tensor", vz2)

            for ii, (i0, iw) in enumerate(ichunks):
                kt_ld = {}

                def load_quarter(q, ph=ph):
                    bsel = q % 2
                    c.waitd("sync", ktbuf_lastrd[bsel])
                    dl = None
                    for chh in range(2):
                        dl = c.dma("sync",
                                   SCR[:, 8192 + 4096 * bsel + 2048 * chh:
                                       8192 + 4096 * bsel + 2048 * (chh + 1)],
                                   g["kT_d"][128 * (2 * ph + chh):128 * (2 * ph + chh + 1),
                                             HALO + 2048 * q:HALO + 2048 * (q + 1)],
                                   sem="ld_kt%d" % bsel)
                    kt_ld[q] = dl

                load_quarter(0)
                load_quarter(1)

                def kt_ap(s, j):
                    q, jj = j // 16, j % 16
                    base = 8192 + 4096 * (q % 2) + 2048 * s
                    return base + 128 * jj

                def emit_qk(s, j):
                    q = j // 16
                    c.waitd("tensor", kt_ld[q])
                    if (s, j - 1) in exp_cnt:
                        c.wait("tensor", "AC", exp_cnt[(s, j - 1)])
                    base = kt_ap(s, j)
                    vp = None
                    for h_i in range(2):
                        r = 64 * h_i
                        vp = c.op("tensor", lambda e, s=s, j=j, r=r, base=base, i0=i0,
                                  iw=iw, ph=ph:
                                  e.matmul(PS[:, 1024 * s + 512 * (r // 64):
                                              1024 * s + 512 * (r // 64) + iw],
                                           SCR[r:r + 64, base:base + 128],
                                           A16[r:r + 64, 2112 * (2 * ph + s) + i0:
                                               2112 * (2 * ph + s) + i0 + iw],
                                           start=True, stop=True, tile_position=(r, 0)))
                    return vp

                def emit_exp(s, j):
                    slot = (j % PD) + PD * s
                    dep = slot_rd.get(slot)
                    if dep is not None:
                        c.wait("scalar", "PE", dep)
                    c.wait("scalar", "PE", qk_cnt[(s, j)])
                    if iw == 512:
                        va = c.op("scalar", lambda e, s=s, j=j, slot=slot:
                                  e.activation(out=SCR[:, 1024 * slot:1024 * slot + 1024],
                                               in_=PS[:, 1024 * s:1024 * s + 1024],
                                               func=AF.Exp,
                                               bias=g["gbC"][:, j:j + 1], scale=1.0))
                    else:
                        for h_i in range(2):
                            va = c.op("scalar", lambda e, s=s, j=j, slot=slot, h_i=h_i,
                                      iw=iw:
                                      e.activation(
                                          out=SCR[:, 1024 * slot + iw * h_i:
                                                  1024 * slot + iw * (h_i + 1)],
                                          in_=PS[:, 1024 * s + 512 * h_i:
                                                 1024 * s + 512 * h_i + iw],
                                          func=AF.Exp,
                                          bias=g["gbC"][:, j:j + 1], scale=1.0))
                    exp_cnt[(s, j)] = va[1]
                    return va

                pv_vph = [0]

                def emit_pv(s, j):
                    slot = (j % PD) + PD * s
                    if pv_vph[0] <= j // 16:
                        c.waitd("tensor", vph_ld[2 * (j // 16) + 1])
                        pv_vph[0] = j // 16 + 1
                    c.wait("tensor", "AC", exp_cnt[(s, j)])
                    vp = None
                    for h_i in range(2):
                        vp = c.op("tensor", lambda e, s=s, j=j, slot=slot, h_i=h_i, iw=iw:
                                  e.matmul(PS[64 * h_i:64 * h_i + 64, 512 * (4 + s):
                                              512 * (4 + s) + iw],
                                           SCR[:, 16384 + 256 * j + 128 * s + 64 * h_i:
                                               16384 + 256 * j + 128 * s + 64 * (h_i + 1)],
                                           SCR[:, 1024 * slot + iw * h_i:
                                               1024 * slot + iw * h_i + iw],
                                           start=False, stop=(j == NJ - 1),
                                           skip_group_check=True,
                                           tile_position=(0, 64 * h_i)))
                    return vp

                def emit_den(j):
                    vp = None
                    for hh in range(4):
                        s, h_i = hh // 2, hh % 2
                        slot = (j % PD) + PD * s
                        cpos = 32 * hh
                        vp = c.op("tensor", lambda e, j=j, s=s, h_i=h_i, slot=slot,
                                  cpos=cpos, iw=iw:
                                  e.matmul(PS[cpos:cpos + 1, 512 * 6:512 * 6 + iw],
                                           g["ones16"][:, 0:1],
                                           SCR[:, 1024 * slot + iw * h_i:
                                               1024 * slot + iw * h_i + iw],
                                           start=False, stop=(j == NJ - 1),
                                           skip_group_check=True,
                                           tile_position=(0, cpos)))
                    return vp

                qk_cnt = {}

                def do_qk(jq):
                    vp = emit_qk(0, jq)
                    qk_cnt[(0, jq)] = vp[1]
                    vp = emit_qk(1, jq)
                    qk_cnt[(1, jq)] = vp[1]
                    if jq % 16 == 15:
                        ktbuf_lastrd[(jq // 16) % 2] = ("PE", vp[1])
                        if jq // 16 + 2 <= 3:
                            load_quarter(jq // 16 + 2)
                    emit_exp(0, jq)
                    emit_exp(1, jq)

                do_qk(0)
                do_qk(1)
                if pending_fin[0] is not None:
                    pending_fin[0]()
                    pending_fin[0] = None
                do_qk(2)
                do_qk(3)
                for j in range(NJ):
                    if j == 0:
                        c.waitd("tensor", mset_dep[0])
                    emit_pv(0, j)
                    emit_pv(1, j)
                    dn = emit_den(j)
                    slot_rd[(j % PD)] = dn[1]
                    slot_rd[(j % PD) + PD] = dn[1]
                    if j + 4 < NJ:
                        do_qk(j + 4)

                c.waitd("vector", dn)
                c.wait("vector", "PE", c.cur("PE"))
                vrec = c.op("vector", lambda e, iw=iw: e.reciprocal(
                    out=g["recip_s"][:, 0:iw], in_=bank(6)[:, 0:iw]))
                c.waitd("gpsimd", vrec)
                d96 = c.dma("gpsimd", g["r96"][0:1, 0:iw],
                            g["recip_s"][96:97, 0:iw], sem="gd_r96")

                def make_fin(i0=i0, iw=iw, ph=ph, dn=dn, vrec=vrec, d96=d96):
                    def fin():
                        aux_cp = None
                        for s in range(2):
                            c.waitd("tensor", vrec)
                            if s == 1:
                                c.waitd("tensor", d96)
                                c.waitd("tensor", aux_cp)
                            srcs = [g["recip_s"][0:1, 0:iw] if s == 0 else
                                    g["recip_s"][64:65, 0:iw],
                                    g["recip_s"][32:33, 0:iw] if s == 0 else
                                    g["r96"][0:1, 0:iw]]
                            obase = [0 if s == 0 else 64, 32 if s == 0 else 0]
                            c.fence_pe()
                            for h_i in range(2):
                                vp = c.op("tensor", lambda e, s=s, h_i=h_i, iw=iw,
                                          srcs=srcs, ob2=obase[h_i]:
                                          e.matmul(PS[64 * h_i:64 * h_i + 64,
                                                      512 * 7:512 * 7 + iw],
                                                   g["onesf"][ob2:ob2 + 1, 0:64],
                                                   srcs[h_i], start=True, stop=True,
                                                   tile_position=(ob2, 64 * h_i)))
                            c.fence_pe()
                            c.waitd("vector", vp)
                            c.wait("vector", "VE", c.cur("VE"))
                            vv = c.op("vector", lambda e, iw=iw: e.tensor_copy(
                                out=g["bc_sb"][:, 0:iw], in_=bank(7)[:, 0:iw]))
                            aux_cp = vv
                            c.waitd("vector", vv)
                            vv2 = c.op("vector", lambda e, s=s, i0=i0, iw=iw, ph=ph:
                                       e.tensor_tensor(
                                           out=B16[:, 2112 * (2 * ph + s) + i0:
                                                   2112 * (2 * ph + s) + i0 + iw],
                                           in0=PS[:, 512 * (4 + s):512 * (4 + s) + iw],
                                           in1=g["bc_sb"][:, 0:iw], op=AL.mult))
                            aoT_dep[0] = vv2
                            c.waitd("vector", vv2)
                            vz = c.op("vector", lambda e, s=s: e.memset(
                                bank(4 + s)[:, :], 0.0))
                            mset_dep[0] = vz
                        c.waitd("vector", d96)
                        c.wait("vector", "VE", c.cur("VE"))
                        vz2 = c.op("vector", lambda e: e.memset(bank(6)[:, :], 0.0))
                        mset_dep[0] = vz2
                    return fin

                pending_fin[0] = make_fin()
            vph_rd[0] = ("PE", c.cur("PE"))
            if MKDBG == 2 and ph == int(os.environ.get("MK_NPH", "4")) - 1:
                c.waitd("gpsimd", aoT_dep[0])
                dd = c.dma("gpsimd", g["dbg1"][:, 0:4224 * (ph + 1)],
                           B16[:, 0:4224 * (ph + 1)], sem="gd_dbg")
                c.waitd("gpsimd", dd)
        if pending_fin[0] is not None:
            pending_fin[0]()
            pending_fin[0] = None
        return aoT_dep[0]

    def p4(ao_dep):
        c.wait("tensor", "VE", c.cur("VE"))
        c.wait("tensor", "AC", c.cur("AC"))
        wdep = load_w("wout", SCR, 0, 8, 1024, deps=(("PE", c.cur("PE")),))
        c.waitd("tensor", wdep)
        c.waitd("tensor", ao_dep)
        ps_rd = [None] * 4
        for k in range(NBLK):
            nr, t0 = blk_nr(k), blk_t0(k)
            for df in range(2):
                pb = (2 * k + df) % 4
                c.waitd("tensor", ps_rd[pb])
                vp = None
                npc = 2 * int(os.environ.get("MK_NPH", "4"))
                for pc in range(npc):
                    vp = c.op("tensor", lambda e, k=k, nr=nr, t0=t0, df=df, pc=pc, pb=pb,
                              npc=npc:
                              e.matmul(bank(pb)[0:nr, :],
                                       B16[:, 2112 * pc + t0:2112 * pc + t0 + nr],
                                       SCR[:, 1024 * pc + 512 * df:1024 * pc + 512 * (df + 1)],
                                       start=(pc == 0), stop=(pc == npc - 1)))
                c.waitd("vector", vp)
                vv = c.op("vector", lambda e, k=k, nr=nr, df=df, pb=pb: e.tensor_tensor(
                    out=x_sb[0:nr, 1024 * k + 512 * df:1024 * k + 512 * (df + 1)],
                    in0=x_sb[0:nr, 1024 * k + 512 * df:1024 * k + 512 * (df + 1)],
                    in1=bank(pb)[0:nr, :], op=AL.add))
                ps_rd[pb] = vv
        return ("VE", c.cur("VE"))

    def p5(p4_dep):
        c.wait("tensor", "VE", c.cur("VE"))
        c.wait("tensor", "AC", c.cur("AC"))
        c.waitd("vector", p4_dep)
        vv = c.op("vector", lambda e: e.tensor_scalar(
            out=x_sb[0:64, 0:1024], in0=x_sb[0:64, 0:1024],
            scalar1=g["hmask_s"][:, :], scalar2=None, op0=AL.mult))
        x2t_dep = [None] * NBLK
        tr_dep = [None] * NBLK

        def apply_cb(k, nr, sc, bi):
            col = 16896 + 1024 * (k % 4)
            if k >= 4:
                c.waitd("vector", tr_dep[k - 4])
            vv = c.op("vector", lambda e, k=k, nr=nr, col=col, sc=sc, bi=bi: e.tensor_scalar(
                out=SCR[0:nr, col:col + 1024], in0=x_sb[0:nr, 1024 * k:1024 * (k + 1)],
                scalar1=stats[0:nr, sc:sc + 1], scalar2=stats[0:nr, bi:bi + 1],
                op0=AL.mult, op1=AL.add))
            dst = SCR[:, 0:16896].rearrange("p (cc t) -> p cc t", cc=8)[
                :, :, blk_t0(k):blk_t0(k) + nr]
            tr_dep[k] = to_x1t(SCR[:, col:col + 1024], nr, dst, vv)
            x2t_dep[k] = tr_dep[k]

        ln_pass(NBLK, blk_nr, lambda k: x_sb[0:blk_nr(k), 1024 * k:1024 * (k + 1)],
                apply_cb, xdeps=(vv,))

        wdep_a = load_w("wka", A16, 0, 8, 1024, deps=(("PE", c.cur("PE")),))
        wdep = load_w("wkb", A16, 8192, 8, 1024)
        c.waitd("tensor", wdep_a)
        c.waitd("tensor", wdep)
        for k in range(NBLK):
            c.waitd("tensor", x2t_dep[k])
        ps_rd = [None] * 4
        kact_dep = None
        for dch in range(8):
            for tt in range(4):
                pb = (4 * dch + tt) % 4
                c.waitd("tensor", ps_rd[pb])
                vp = None
                for cc in range(8):
                    vp = c.op("tensor", lambda e, dch=dch, tt=tt, cc=cc, pb=pb:
                              e.matmul(bank(pb)[:, :],
                                       A16[:, 1024 * cc + 128 * dch:1024 * cc + 128 * (dch + 1)],
                                       X1T(cc, HALO + 512 * tt, 512),
                                       start=(cc == 0), stop=False))
                    vp = c.op("tensor", lambda e, dch=dch, tt=tt, cc=cc, pb=pb:
                              e.matmul(bank(pb)[:, :],
                                       A16[:, 8192 + 1024 * cc + 128 * dch:
                                           8192 + 1024 * cc + 128 * (dch + 1)],
                                       X1T(cc, HALO - 1 + 512 * tt, 512),
                                       start=False, stop=(cc == 7)))
                c.waitd("vector", vp)
                c.wait("vector", "VE", c.cur("VE"))
                vv = c.op("vector", lambda e, pb=pb: e.tensor_scalar_max(
                    out=g["bc_sb"][:, 0:512], in0=bank(pb)[:, :], scalar1=0.0))
                c.waitd("vector", vv)
                vv = c.op("vector", lambda e, dch=dch, tt=tt: e.tensor_tensor(
                    out=SCR[:, 16896 + 2048 * dch + 512 * tt:16896 + 2048 * dch + 512 * (tt + 1)],
                    in0=g["bc_sb"][:, 0:512], in1=g["bc_sb"][:, 0:512], op=AL.mult))
                ps_rd[pb] = vv
                kact_dep = vv

        wdep_a = load_w("wra", A16, 0, 8, 1024, deps=(("PE", c.cur("PE")),))
        wdep = load_w("wrb", A16, 8192, 8, 1024)
        wdep2 = load_w("wvf", B16, 0, 8, 1024)
        c.waitd("tensor", wdep_a)
        c.waitd("tensor", wdep)
        c.waitd("tensor", wdep2)
        c.waitd("tensor", kact_dep)
        rdep = {}
        vdep = {}
        for tc in range(16):
            pr = (tc % 2) * 4
            c.waitd("tensor", rdep.get(tc - 2))
            vp = None
            for df in range(2):
                for cc in range(8):
                    vp = c.op("tensor", lambda e, tc=tc, df=df, cc=cc, pr=pr:
                              e.matmul(bank(pr + df)[:, :],
                                       SCR[:, 2112 * cc + HALO + 128 * tc:
                                           2112 * cc + HALO + 128 * (tc + 1)],
                                       A16[:, 1024 * cc + 512 * df:1024 * cc + 512 * (df + 1)],
                                       start=(cc == 0), stop=False))
                    vp = c.op("tensor", lambda e, tc=tc, df=df, cc=cc, pr=pr:
                              e.matmul(bank(pr + df)[:, :],
                                       SCR[:, 2112 * cc + HALO - 1 + 128 * tc:
                                           2112 * cc + HALO - 1 + 128 * (tc + 1)],
                                       A16[:, 8192 + 1024 * cc + 512 * df:
                                           8192 + 1024 * cc + 512 * (df + 1)],
                                       start=False, stop=(cc == 7)))
            c.waitd("scalar", vp)
            rcol = 4096 + 1024 * (tc % 2)
            c.waitd("scalar", rdep.get(tc - 2))
            va = c.op("scalar", lambda e, pr=pr, rcol=rcol: e.activation(
                out=B16F[:, rcol:rcol + 1024], in_=bank(pr, 2)[:, 0:1024], func=AF.Sigmoid))
            rdep[tc] = va
            c.waitd("tensor", vdep.get(tc - 2))
            vp = None
            for df in range(2):
                for cc in range(8):
                    vp = c.op("tensor", lambda e, tc=tc, df=df, cc=cc, pr=pr:
                              e.matmul(bank(pr + 2 + df)[:, :],
                                       SCR[:, 16896 + 2048 * cc + 128 * tc:
                                           16896 + 2048 * cc + 128 * (tc + 1)],
                                       B16[:, 1024 * cc + 512 * df:1024 * cc + 512 * (df + 1)],
                                       start=(cc == 0), stop=(cc == 7)))
            c.waitd("vector", vp)
            c.waitd("vector", va)
            scol = 6144 + 1024 * (tc % 2)
            vv = c.op("vector", lambda e, pr=pr, rcol=rcol, scol=scol: e.scalar_tensor_tensor(
                out=B16F[:, scol:scol + 1024], in0=bank(pr + 2, 2)[:, 0:1024], scalar=1.0,
                in1=B16F[:, rcol:rcol + 1024], op0=AL.mult, op1=AL.mult))
            vdep[tc] = vv
            c.waitd("vector", vv)
            kb = 1 + tc
            vv = c.op("vector", lambda e, kb=kb, scol=scol: e.tensor_tensor(
                out=x_sb[:, 1024 * kb:1024 * (kb + 1)],
                in0=x_sb[:, 1024 * kb:1024 * (kb + 1)],
                in1=B16F[:, scol:scol + 1024], op=AL.add))
            vdep[tc] = vv
            rdep[tc] = vv
        return ("VE", c.cur("VE"))

    P1N = int(os.environ.get("MK_P1", "4"))
    p0_main()
    g_slotst = P0ST["g_slotst"]
    p1a()
    if P1N >= 2:
        p1b()
    if P1N >= 3:
        p1c()
    if P1N >= 4:
        xw_dep = p1d()
    else:
        xw_dep = ("VE", c.cur("VE"))
    if PHASES >= 2:
        p2_done = p2()
    if PHASES >= 3:
        ao_dep = p3(p2_done)
        if MKDBG == 1:
            c.waitd("gpsimd", ao_dep)
            dbgd = c.dma("gpsimd", g["dbg1"][:, 0:8448], B16[:, 0:8448], sem="gd_dbg")
            c.waitd("gpsimd", dbgd)
            dbgd = c.dma("gpsimd", g["dbg1"][:, 8448:16896], B16[:, 8448:16896],
                         sem="gd_dbg")
            c.waitd("gpsimd", dbgd)
        x4_dep = p4(ao_dep)
    else:
        x4_dep = xw_dep
    if PHASES >= 5:
        x5_dep = p5(x4_dep)
    else:
        x5_dep = x4_dep

    c.waitd("gpsimd", x5_dep)
    dst = c.dma("gpsimd", g["y"].rearrange("(k p) m -> p k m", p=128),
                x_sb[:, 1024:].rearrange("p (k m) -> p k m", k=16), sem="gd_y")
    c.waitd("gpsimd", dst)


_CACHED = {}


def kernel(L0_attn_x, L0_attn_w_v, L0_attn_b_v, L0_attn_w_wqk, L0_attn_b_wqk,
           L0_attn_w_qk, L0_attn_b_qk, L0_attn_w_out, L0_attn_b_out,
           L0_attn_w_pos_embedding, L0_attn_pos_embedding,
           L0_ffn_W_r, L0_ffn_W_k, L0_ffn_W_v, L0_ffn_mu_r, L0_ffn_mu_k):
    import ml_dtypes
    x = np.ascontiguousarray(L0_attn_x)
    pos = L0_attn_pos_embedding
    wpos = L0_attn_w_pos_embedding
    ident = np.eye(128).astype(ml_dtypes.bfloat16)
    wb = np.ascontiguousarray(wpos[WS - 1 - np.arange(WS)])
    wb128 = np.concatenate([wb, wb]).reshape(128, 1).astype(np.float32)

    nc = build_program()
    in_maps = []
    for core in range(8):
        b, qc = core // 4, core % 4
        start = 2048 * qc
        if qc == 0:
            halo = np.zeros((HALO, D), np.float32)
            hm = np.zeros((64, 1), np.float32)
        else:
            halo = x[b, start - HALO:start]
            hm = np.ones((64, 1), np.float32)
        x_ownh = np.ascontiguousarray(np.concatenate([halo, x[b, start:start + O]], axis=0))
        x_rest = np.ascontiguousarray(
            np.concatenate([x[b, :start], x[b, start + O:]], axis=0))
        perm = np.concatenate([np.arange(start, start + O),
                               np.arange(0, start),
                               np.arange(start + O, L)])
        gidx = np.clip(L - 1 - perm, 0, pos.shape[0] - 1)
        gbias = pos[gidx]
        gb_dev = np.ascontiguousarray(gbias.reshape(64, 128).T).astype(np.float32)
        in_maps.append({
            "x_ownh": x_ownh.astype(np.float32),
            "x_rest": x_rest.astype(np.float32),
            "w_v": L0_attn_w_v, "b_v": L0_attn_b_v,
            "w_wqk": L0_attn_w_wqk, "b_wqk": L0_attn_b_wqk,
            "w_qk": L0_attn_w_qk, "b_qk": L0_attn_b_qk,
            "w_out": L0_attn_w_out, "b_out": L0_attn_b_out,
            "f_wr": L0_ffn_W_r, "f_wk": L0_ffn_W_k, "f_wv": L0_ffn_W_v,
            "mu_r": L0_ffn_mu_r, "mu_k": L0_ffn_mu_k,
            "gb": gb_dev, "wbv": wb128, "hmask": hm, "ident": ident,
        })
    res = run_bass_kernel_spmd(nc, in_maps, core_ids=list(range(8)))
    out = np.empty((B, L, D), np.float32)
    for core in range(8):
        b, qc = core // 4, core % 4
        out[b, 2048 * qc:2048 * (qc + 1)] = res.results[core]["y"]
    return out
